# version 01
# baseline (speedup 1.0000x reference)
import numpy as np
import ml_dtypes
from contextlib import ExitStack

import concourse.bass as bass
import concourse.mybir as mybir
from concourse.bass_utils import run_bass_kernel_spmd

F32 = mybir.dt.float32
BF16 = mybir.dt.bfloat16
I32 = mybir.dt.int32
AF = mybir.ActivationFunctionType
ALU = mybir.AluOpType
AX = mybir.AxisListType
BF = ml_dtypes.bfloat16

FRONT_OTILES = None
DBG_SSD = False
DBG_B = False
C_STAGES = ("ffn1", "attn")
A_STAGES = ("setup", "proj", "gpass", "vtr", "logg", "core")
DBG_C = False

B, L, D = 2, 4096, 2048
TOK = 1024
TT = 8
NCH = 4
CH = 256
H0, P0_, N0 = 32, 128, 64
DIN = 4096
CONVD = 4224
DTOFF = 8320
DIP = 8352
FF0 = 8192
FF1 = 5632
HQ, R, DH = 8, 2, 128
GATE_OFFSET = 6.906768


class Ctx:
    ENGINES = ("sync", "vector", "scalar", "tensor", "gpsimd")

    DRAINY = ("vector", "scalar", "gpsimd")

    def __init__(self, nc, es):
        self.nc = nc
        self.es = es
        self.ops = {e: [] for e in self.ENGINES}
        self.sems = {}
        self.scopes = []
        self.dirty = {e: False for e in self.ENGINES}
        self._dma_throttle = {}

    def _maybe_drain(self, eng):
        if eng in self.DRAINY and self.dirty[eng]:
            self.ops[eng].append(lambda e: e.drain())
            self.dirty[eng] = False

    def sem(self, name):
        if name not in self.sems:
            h = self.es.enter_context(self.nc.semaphore(name))
            self.sems[name] = [h, 0]
        return self.sems[name][0]

    def bump(self, name, n):
        self.sem(name)
        self.sems[name][1] += n
        return self.sems[name][1]

    def count(self, name):
        self.sem(name)
        return self.sems[name][1]

    def op(self, eng, fn, inc=None, compute=True):
        self._maybe_drain(eng)
        if inc is not None:
            nm, n = inc
            h = self.sem(nm)
            self.bump(nm, n)
            def wrapped(e, fn=fn, h=h, n=n):
                ins = fn(e)
                ins.then_inc(h, n)
            self.ops[eng].append(wrapped)
        else:
            self.ops[eng].append(fn)
        if compute and eng in self.DRAINY:
            self.dirty[eng] = True
        return self.count(inc[0]) if inc else None

    def wait(self, eng, name, val):
        h = self.sem(name)
        self.ops[eng].append(lambda e, h=h, v=val: e.wait_ge(h, v))

    def wait_now(self, eng, name):
        self.wait(eng, name, self.count(name))

    def dma(self, eng, out, in_, sem="dma", **kw):
        key = (eng, sem)
        cur = self.count(sem)
        last = self._dma_throttle.get(key, 0)
        if cur - last >= 32:
            self.wait(eng, sem, cur)
            self._dma_throttle[key] = cur
        return self.op(eng, lambda e, o=out, i=in_, kw=kw: e.dma_start(out=o, in_=i, **kw),
                       inc=(sem, 16), compute=False)

    def sb(self, name, shape, dt):
        return self.es.enter_context(self.nc.sbuf_tensor(name, shape, dt))

    def ps(self, name, shape, dt):
        return self.es.enter_context(self.nc.psum_tensor(name, shape, dt))

    def scope(self):
        return ExitStack()

    def emit(self):
        with self.nc.Block() as block:
            for eng in self.ENGINES:
                lst = self.ops[eng]
                if not lst:
                    continue
                def body(e, lst=lst):
                    for f in lst:
                        f(e)
                getattr(block, eng)(body)
        self.ops = {e: [] for e in self.ENGINES}


def _bf16(x):
    return np.ascontiguousarray(x.astype(BF))


def _f32(x):
    return np.ascontiguousarray(np.asarray(x, dtype=np.float32))


def host_constants():
    c = {}
    c["idn128"] = np.eye(128, dtype=BF)
    c["idn128f"] = np.eye(128, dtype=np.float32)
    c["ones1x128f"] = np.ones((1, 128), dtype=np.float32)
    c["onescol_bf"] = np.ones((128, 1), dtype=BF)
    j = np.arange(128)[:, None]
    i = np.arange(256)[None, :]
    m0 = (i >= j).astype(np.float32)
    m1 = (i >= j + 128).astype(np.float32)
    c["ssdmask"] = np.ascontiguousarray(np.concatenate([m0, m1], axis=1))
    sp = np.arange(128)[:, None]
    tq = np.arange(512)[None, :]
    masks = [((o * 128 + sp) <= tq).astype(np.float32) for o in range(4)]
    c["attnmask"] = np.ascontiguousarray(np.concatenate(masks, axis=1).astype(BF))
    return c


def host_prep(inputs):
    u = _f32(inputs["L0_attn_u"])
    wip = _f32(inputs["L0_attn_in_proj_w"]).T
    conv_w = _f32(inputs["L0_attn_conv_w"])
    conv_b = _f32(inputs["L0_attn_conv_b"])
    A_log = _f32(inputs["L0_attn_A_log"])
    dt_bias = _f32(inputs["L0_attn_dt_bias"])
    D_skip = _f32(inputs["L0_attn_D_skip"])
    norm_w = _f32(inputs["L0_attn_norm_w"])
    wo = _f32(inputs["L0_attn_o_proj_w"]).T
    wg0 = _f32(inputs["L0_ffn_W_g"]).T
    wu0 = _f32(inputs["L0_ffn_W_u"]).T
    wd0 = _f32(inputs["L0_ffn_W_d"]).T
    wq = _f32(inputs["L1_attn_Wq"])
    wk = _f32(inputs["L1_attn_Wk"])
    wv = _f32(inputs["L1_attn_Wv"])
    wg1h = _f32(inputs["L1_attn_Wg"])
    wc = _f32(inputs["L1_attn_Wc"])
    bc = _f32(inputs["L1_attn_bc"])
    gate_w = _f32(inputs["L1_ffn_gate_w"]).T
    up_w = _f32(inputs["L1_ffn_up_w"]).T
    down_w = _f32(inputs["L1_ffn_down_w"]).T
    gate_s = _f32(inputs["L1_ffn_gate_s"])
    up_s = _f32(inputs["L1_ffn_up_s"])
    down_s = _f32(inputs["L1_ffn_down_s"])

    consts = host_constants()
    conv_w_t = np.zeros((128, 34 * 4), dtype=np.float32)
    conv_b_t = np.zeros((128, 34), dtype=np.float32)
    for gi in range(32):
        conv_w_t[:, gi * 4:(gi + 1) * 4] = conv_w[gi * 128:(gi + 1) * 128]
        conv_b_t[:, gi] = conv_b[gi * 128:(gi + 1) * 128]
    conv_w_t[0:64, 32 * 4:33 * 4] = conv_w[4096:4160]
    conv_b_t[0:64, 32] = conv_b[4096:4160]
    conv_w_t[0:64, 33 * 4:34 * 4] = conv_w[4160:4224]
    conv_b_t[0:64, 33] = conv_b[4160:4224]
    normw_bc = np.ascontiguousarray(np.broadcast_to(norm_w, (128, 4096)))
    D_bc = np.ascontiguousarray(np.broadcast_to(D_skip, (128, 32)))
    bc_bc = np.ascontiguousarray(np.broadcast_to(bc, (128, 2048)))
    downs_bc = np.ascontiguousarray(np.broadcast_to(down_s, (128, 2048)))
    gates_t = np.ascontiguousarray(gate_s.reshape(44, 128).T)
    ups_t = np.ascontiguousarray(up_s.reshape(44, 128).T)

    in_maps = []
    for k in range(8):
        b, j = k // 4, k % 4
        t0 = j * TOK
        xl = u[b, t0:t0 + TOK]
        if j == 0:
            halo = np.zeros((3, D), dtype=np.float32)
        else:
            halo = u[b, t0 - 3:t0]
        halo128 = np.ascontiguousarray(np.concatenate([xl[899:1024], halo], axis=0))
        Mmat = np.zeros((8, 8), dtype=np.float32)
        inc = np.zeros((8,), dtype=np.float32)
        me = b * 4 + j
        for jsrc in range(b * 4, me):
            inc[jsrc] = 1.0
            for m in range(jsrc + 1, me):
                Mmat[jsrc, m] = 1.0
        Mbig = np.ascontiguousarray(np.broadcast_to(Mmat.reshape(1, 64), (32, 64)))
        incbig = np.ascontiguousarray(np.broadcast_to(inc, (32, 8)))
        im = dict(
            x_loc=np.ascontiguousarray(xl), x_halo=halo128,
            wip=wip, conv_w_t=conv_w_t, conv_b_t=conv_b_t,
            A_log=A_log.reshape(32, 1), dt_bias=dt_bias.reshape(32, 1),
            D_bc=D_bc, normw_bc=normw_bc, wo=wo,
            wg0=wg0, wu0=wu0, wd0=wd0,
            wq_loc=np.ascontiguousarray(wq[:, j * 512:(j + 1) * 512]),
            wg_loc=np.ascontiguousarray(wg1h[:, 2 * j:2 * j + 2]),
            wk_loc=np.ascontiguousarray(wk[:, j * 256:(j + 1) * 256]),
            wv_loc=np.ascontiguousarray(wv[:, j * 256:(j + 1) * 256]),
            wg1h=wg1h, wc=wc, bc_bc=bc_bc,
            gate_w=gate_w, up_w=up_w, down_w=down_w,
            gates_t=gates_t, ups_t=ups_t, downs_bc=downs_bc,
            Mbig=Mbig, incbig=incbig,
            **consts,
        )
        in_maps.append(im)
    return in_maps



def decl_in(nc, name, shape, dt):
    return nc.dram_tensor(name, shape, dt, kind="ExternalInput").ap()


def decl_out(nc, name, shape, dt):
    return nc.dram_tensor(name, shape, dt, kind="ExternalOutput").ap()


def decl_scratch(nc, name, shape, dt, external):
    kind = "ExternalOutput" if external else "Internal"
    if external:
        return nc.dram_tensor(name, shape, dt, kind="ExternalOutput").ap()
    return nc.dram_tensor(name, shape, dt).ap()


def emit_P0_ln_quant(ctx, g, scope):
    nc = ctx.nc
    x_loc, x_halo = g["x_loc"], g["x_halo"]
    xnT = g["xnT"]
    xq_d = g["xq_d"]
    idn = g["idn"]

    sb = lambda n, s, d: scope.enter_context(nc.sbuf_tensor(n, s, d))
    xt = sb("p0_x", [128, 2048], F32)
    sq = sb("p0_sq", [128, 2048], F32)
    xr = sb("p0_xr", [128, 2048], F32)
    xqi = sb("p0_xqi", [128, 2048], I32)
    xn16 = sb("p0_xn16", [128, 2048], BF16)
    xq16 = sb("p0_xq16", [128, 2048], BF16)
    st = sb("p0_st", [128, 12], F32)
    ptr0 = scope.enter_context(nc.psum_tensor("p0_ptr0", [128, 1024], BF16))
    ptr1 = scope.enter_context(nc.psum_tensor("p0_ptr1", [128, 1024], BF16))
    ptrs = (ptr0, ptr1)

    for t in range(9):
        rows = 128
        src = x_loc[t * 128:(t + 1) * 128, :] if t < 8 else x_halo
        ctx.wait_now("sync", "p0v")
        ctx.wait_now("sync", "p0s")
        ctx.dma("sync", xt[0:rows, :], src, sem="p0ld")
        ctx.wait_now("vector", "p0ld")
        ctx.op("vector", lambda v, r=rows: v.reduce_sum(out=st[0:r, 0:1], in_=xt[0:r, :], axis=AX.X))
        ctx.wait_now("scalar", "p0ld")
        ctx.op("scalar", lambda s, r=rows: s.activation(out=sq[0:r, :], in_=xt[0:r, :], func=AF.Square,
                                                        accum_out=st[0:r, 1:2]), inc=("p0s", 1))
        ctx.wait_now("vector", "p0s")
        ctx.op("vector", lambda v, r=rows: v.tensor_scalar(out=st[0:r, 0:1], in0=st[0:r, 0:1],
                                                           scalar1=1.0 / 2048, scalar2=None, op0=ALU.mult))
        ctx.op("vector", lambda v, r=rows: v.tensor_scalar(out=st[0:r, 2:3], in0=st[0:r, 1:2],
                                                           scalar1=1.0 / 2048, scalar2=None, op0=ALU.mult))
        ctx.op("vector", lambda v, r=rows: v.tensor_tensor(out=st[0:r, 3:4], in0=st[0:r, 0:1],
                                                           in1=st[0:r, 0:1], op=ALU.mult))
        ctx.op("vector", lambda v, r=rows: v.tensor_scalar(out=st[0:r, 2:3], in0=st[0:r, 2:3],
                                                           scalar1=st[0:r, 3:4], scalar2=1e-5,
                                                           op0=ALU.subtract, op1=ALU.add), inc=("p0v", 1))
        ctx.wait_now("scalar", "p0v")
        ctx.op("scalar", lambda s, r=rows: s.sqrt(out=st[0:r, 2:3], in_=st[0:r, 2:3]), inc=("p0s", 1))
        ctx.wait_now("vector", "p0s")
        ctx.op("vector", lambda v, r=rows: v.reciprocal(out=st[0:r, 2:3], in_=st[0:r, 2:3]))
        ctx.op("vector", lambda v, r=rows: v.tensor_scalar(out=xr[0:r, :], in0=xt[0:r, :],
                                                           scalar1=st[0:r, 0:1], scalar2=st[0:r, 2:3],
                                                           op0=ALU.subtract, op1=ALU.mult))
        ctx.op("vector", lambda v, r=rows: v.tensor_copy(out=xn16[0:r, :], in_=xr[0:r, :]), inc=("p0v", 1))

        ctx.wait_now("tensor", "p0v")
        for f in range(16):
            pt = ptrs[f % 2]
            ctx.op("tensor", lambda te, r=rows, f=f, pt=pt: te.transpose(
                pt[:, 0:r], xn16[0:r, f * 128:(f + 1) * 128],
                idn[0:r, 0:r]), inc=("p0t", 1))
            ctx.wait_now("vector", "p0t")
            dst = (t * 128) if t < 8 else 899
            ctx.op("vector", lambda v, r=rows, f=f, pt=pt, dst=dst: v.tensor_copy(
                out=xnT[:, f * 1027 + dst: f * 1027 + dst + r],
                in_=pt[:, 0:r]), inc=("p0v", 1))
            if f < 15:
                ctx.wait_now("tensor", "p0v")

        if t == 8:
            break

        ctx.op("vector", lambda v: v.reduce_sum(out=st[:, 4:5], in_=xr[:], axis=AX.X), inc=("p0v", 1))
        ctx.wait_now("scalar", "p0v")
        ctx.op("scalar", lambda s: s.activation(out=sq[:], in_=xr[:], func=AF.Square,
                                                accum_out=st[:, 5:6]), inc=("p0s", 1))
        ctx.wait_now("vector", "p0s")
        ctx.op("vector", lambda v: v.tensor_scalar(out=st[:, 4:5], in0=st[:, 4:5],
                                                   scalar1=1.0 / 2048, scalar2=None, op0=ALU.mult))
        ctx.op("vector", lambda v: v.tensor_scalar(out=st[:, 6:7], in0=st[:, 5:6],
                                                   scalar1=1.0 / 2048, scalar2=None, op0=ALU.mult))
        ctx.op("vector", lambda v: v.tensor_tensor(out=st[:, 7:8], in0=st[:, 4:5],
                                                   in1=st[:, 4:5], op=ALU.mult))
        ctx.op("vector", lambda v: v.tensor_scalar(out=st[:, 6:7], in0=st[:, 6:7],
                                                   scalar1=st[:, 7:8], scalar2=1e-8,
                                                   op0=ALU.subtract, op1=ALU.add), inc=("p0v", 1))
        ctx.wait_now("scalar", "p0v")
        ctx.op("scalar", lambda s: s.sqrt(out=st[:, 6:7], in_=st[:, 6:7]), inc=("p0s", 1))
        ctx.wait_now("vector", "p0s")
        ctx.op("vector", lambda v: v.reciprocal(out=st[:, 6:7], in_=st[:, 6:7]))
        ctx.op("vector", lambda v: v.tensor_scalar(out=xr[:], in0=xr[:],
                                                   scalar1=st[:, 4:5], scalar2=st[:, 6:7],
                                                   op0=ALU.subtract, op1=ALU.mult))
        ctx.op("vector", lambda v: v.reduce_max(out=st[:, 8:9], in_=xr[:], axis=AX.X,
                                                apply_absolute_value=True))
        ctx.op("vector", lambda v: v.reciprocal(out=st[:, 9:10], in_=st[:, 8:9]))
        ctx.op("vector", lambda v: v.tensor_scalar(out=st[:, 9:10], in0=st[:, 9:10],
                                                   scalar1=127.0, scalar2=None, op0=ALU.mult))
        ctx.op("vector", lambda v: v.tensor_scalar(out=st[:, 10:11], in0=st[:, 8:9],
                                                   scalar1=1.0 / 127.0, scalar2=None, op0=ALU.mult))
        ctx.op("vector", lambda v: v.tensor_scalar(out=xr[:], in0=xr[:],
                                                   scalar1=st[:, 9:10], scalar2=None,
                                                   op0=ALU.mult), inc=("p0v", 1))
        ctx.wait_now("scalar", "p0v")
        ctx.op("scalar", lambda s: s.copy(out=xqi[:], in_=xr[:]))
        ctx.op("scalar", lambda s: s.copy(out=xr[:], in_=xqi[:]), inc=("p0s", 1))
        ctx.wait_now("vector", "p0s")
        ctx.wait_now("vector", "p0sp")
        ctx.op("vector", lambda v: v.tensor_scalar(out=xq16[:], in0=xr[:],
                                                   scalar1=st[:, 10:11], scalar2=None,
                                                   op0=ALU.mult), inc=("p0v", 1))
        ctx.wait_now("gpsimd", "p0v")
        ctx.dma("gpsimd", xq_d[t * 128:(t + 1) * 128, :], xq16[:], sem="p0sp")
    for e in Ctx.ENGINES:
        for s in ("p0sp", "p0v", "p0s", "p0t", "p0ld"):
            ctx.wait_now(e, s)


def emit_load_consts(ctx, pairs):
    for (dst, src) in pairs:
        ctx.dma("sync", dst, src, sem="cld")
    for e in Ctx.ENGINES:
        ctx.wait_now(e, "cld")


def emit_P2_l0attn_front(ctx, g, scope):
    nc = ctx.nc
    xnT = g["xnT"]
    wip_d = g["wip_d"]
    idn, idnf = g["idn"], g["idnf"]
    convw, convb = g["convw_sb"], g["convb_sb"]
    onesf = g["onesf_sb"]
    ssdmask = g["ssdmask_sb"]
    Dbc = g["Dbc_sb"]
    xc_d, y_d, sigz_d, st_d, a_d, bc_d = g["xc_d"], g["y_d"], g["sigz_d"], g["st_d"], g["a_d"], g["bc_d"]
    ccA_in = g["ccA_in"]

    sb = lambda n, s, d: scope.enter_context(nc.sbuf_tensor(n, s, d))
    b16 = sb("b16", [64, 1024], BF16)
    c16 = sb("c16", [64, 1024], BF16)
    Btm = sb("Btm", [128, 8 * 64], BF16)
    dt32 = sb("dt32", [32, 1024], F32)
    a32 = sb("a32", [32, 1024], F32)
    aT = sb("aT", [128, 256], F32)
    dtT = sb("dtT", [128, 256], F32)
    ddT = sb("ddT", [128, 256], F32)
    alast = sb("alast", [32, 8], F32)
    vsc = sb("vsc", [32, 1024], F32)
    vsc2 = sb("vsc2", [32, 1024], F32)
    avec = sb("avec", [32, 2], F32)
    zeros256 = sb("zeros256", [32, 256], F32)

    ctx.op("vector", lambda v: v.memset(zeros256[:], 0.0), inc=("ze", 1))

    with ctx.scope() as s2:
        sb2 = lambda n, s, d: s2.enter_context(nc.sbuf_tensor(n, s, d))
        w32 = [sb2("w32_0", [128, 16 * 128], F32), sb2("w32_1", [128, 16 * 128], F32)]
        w16 = [sb2("w16_0", [128, 16 * 128], BF16), sb2("w16_1", [128, 16 * 128], BF16)]
        xbc32 = sb2("xbc32", [128, 1027], F32)
        cacc = sb2("cacc", [128, 1024], F32)
        xbcs16 = sb2("xbcs16", [128, 1024], BF16)
        xcbuf = sb2("xcbuf", [128, 1024], BF16)
        ps = ctx.ps
        pA = s2.enter_context(nc.psum_tensor("pA", [128, 512], F32))
        pB = s2.enter_context(nc.psum_tensor("pB", [128, 512], F32))
        pH = s2.enter_context(nc.psum_tensor("pH", [128, 512], F32))
        pT0 = s2.enter_context(nc.psum_tensor("pT0", [128, 1024], BF16))
        pT1 = s2.enter_context(nc.psum_tensor("pT1", [128, 1024], BF16))
        pTf0 = s2.enter_context(nc.psum_tensor("pTf0", [128, 512], F32))
        pTf1 = s2.enter_context(nc.psum_tensor("pTf1", [128, 512], F32))
        pTs = (pT0, pT1)

        otiles = [(DTOFF, 32, "dt"), (8192, 64, "b"), (8256, 64, "c")] + \
                 [(4096 + i * 128, 128, "x%d" % i) for i in range(32)]
        if FRONT_OTILES is not None:
            otiles = otiles[:FRONT_OTILES]
        mm_done = [0, 0]
        for idx, (o0, ow, tag) in enumerate(otiles):
            slot = idx % 2
            ctx.wait(
                "sync", "mm", mm_done[slot])
            ctx.dma("sync", w32[slot][:].rearrange("p (c o) -> p c o", o=128)[:, :, 0:ow],
                    wip_d[:, o0:o0 + ow].rearrange("(c p) o -> p c o", p=128), sem="wl%d" % slot)
            ctx.wait_now("vector", "wl%d" % slot)
            ctx.wait(
                "vector", "mm", mm_done[slot])
            ctx.op("vector", lambda v, s=slot, ow=ow: v.tensor_copy(
                out=w16[s][:].rearrange("p (c o) -> p c o", o=128)[:, :, 0:ow],
                in_=w32[s][:].rearrange("p (c o) -> p c o", o=128)[:, :, 0:ow]), inc=("wc", 1))
            ctx.wait_now("tensor", "wc")
            ctx.wait_now("tensor", "ev")
            for tg in range(2):
                pdst = (pA, pB)[tg]
                last = (tag == "dt" and tg == 1)
                for f in range(16):
                    ctx.op("tensor", lambda te, s=slot, f=f, tg=tg, ow=ow, pdst=pdst: te.matmul(
                        pdst[0:ow, :], w16[s][:, f * 128:f * 128 + ow],
                        xnT[:, f * 1027 + tg * 512: f * 1027 + tg * 512 + 512],
                        start=(f == 0), stop=(f == 15)),
                        inc=("mm", 1) if (last and f == 15) else None)
            if tag != "dt":
                for f in range(16):
                    ctx.op("tensor", lambda te, s=slot, f=f, ow=ow: te.matmul(
                        pH[0:ow, 0:3], w16[s][:, f * 128:f * 128 + ow],
                        xnT[:, f * 1027 + 1024: f * 1027 + 1027],
                        start=(f == 0), stop=(f == 15)), inc=("mm", 1) if f == 15 else None)
            mm_done[slot] = ctx.count("mm")

            if tag == "dt":
                ctx.wait_now("scalar", "mm")
                ctx.op("scalar", lambda s: s.copy(out=vsc[:, 0:512], in_=pA[0:32, :]))
                ctx.op("scalar", lambda s: s.copy(out=vsc[:, 512:1024], in_=pB[0:32, :]), inc=("ev", 1))
                emit_P2_dtpost(ctx, g, dict(vsc=vsc, vsc2=vsc2, dt32=dt32, a32=a32, aT=aT,
                                            dtT=dtT, ddT=ddT, alast=alast, avec=avec,
                                            zeros256=zeros256, pTf0=pTf0, pTf1=pTf1, idnf=idnf,
                                            a_d=a_d))
            else:
                ctx.wait_now("scalar", "mm")
                ctx.op("scalar", lambda s, ow=ow: s.copy(out=xbc32[0:ow, 3:515], in_=pA[0:ow, :]))
                ctx.op("scalar", lambda s, ow=ow: s.copy(out=xbc32[0:ow, 515:1027], in_=pB[0:ow, :]))
                ctx.op("scalar", lambda s, ow=ow: s.copy(out=xbc32[0:ow, 0:3], in_=pH[0:ow, 0:3]),
                       inc=("ev", 1))
                ci = {"b": 32, "c": 33}.get(tag, None)
                if ci is None:
                    ci = (o0 - 4096) // 128
                crow = 0
                ctx.wait_now("vector", "ev")
                ctx.wait_now("vector", "cs")
                ctx.op("vector", lambda v, ci=ci, cr=crow, ow=ow: v.tensor_scalar(
                    out=cacc[0:ow, :], in0=xbc32[0:ow, 0:1024],
                    scalar1=convw[cr:cr + ow, ci * 4:ci * 4 + 1],
                    scalar2=None, op0=ALU.mult))
                for k in (1, 2, 3):
                    ctx.op("vector", lambda v, ci=ci, cr=crow, ow=ow, k=k: v.scalar_tensor_tensor(
                        out=cacc[0:ow, :], in0=xbc32[0:ow, k:k + 1024],
                        scalar=convw[cr:cr + ow, ci * 4 + k:ci * 4 + k + 1],
                        in1=cacc[0:ow, :], op0=ALU.mult, op1=ALU.add),
                        inc=("cv", 1) if k == 3 else None)
                ctx.wait_now("scalar", "cv")
                ctx.wait_now("scalar", "tp")
                dstt = {"b": b16, "c": c16}.get(tag, xbcs16)
                ctx.op("scalar", lambda s, ci=ci, cr=crow, ow=ow, dstt=dstt: s.activation(
                    out=dstt[0:ow, :], in_=cacc[0:ow, :], func=AF.Silu,
                    bias=convb[cr:cr + ow, ci:ci + 1]), inc=("cs", 1))
                if tag == "b":
                    ctx.wait_now("gpsimd", "cs")
                    ctx.dma("gpsimd", bc_d[0:64, :], b16[:], sem="sp")
                    ctx.wait_now("tensor", "cs")
                    for cq in range(8):
                        pt = pTs[cq % 2]
                        ctx.op("tensor", lambda te, cq=cq, pt=pt: te.transpose(
                            pt[0:128, 0:64], b16[:, cq * 128:(cq + 1) * 128], idn[0:64, 0:64]),
                            inc=("tp", 1))
                        ctx.wait_now("vector", "tp")
                        ctx.op("vector", lambda v, cq=cq, pt=pt: v.tensor_copy(
                            out=Btm[:, cq * 64:(cq + 1) * 64], in_=pt[0:128, 0:64]), inc=("te", 1))
                        ctx.wait_now("tensor", "te")
                elif tag == "c":
                    ctx.wait_now("gpsimd", "cs")
                    ctx.dma("gpsimd", bc_d[64:128, :], c16[:], sem="sp")
                else:
                    ctx.wait_now("tensor", "cs")
                    ctx.wait_now("vector", "xs")
                    for tt in range(8):
                        pt = pTs[tt % 2]
                        ctx.op("tensor", lambda te, tt=tt, pt=pt: te.transpose(
                            pt[:, 0:128], xbcs16[:, tt * 128:(tt + 1) * 128], idn[:]),
                            inc=("tp", 1))
                        ctx.wait_now("vector", "tp")
                        ctx.op("vector", lambda v, tt=tt, pt=pt: v.tensor_copy(
                            out=xcbuf[:, tt * 128:(tt + 1) * 128], in_=pt[:, 0:128]), inc=("te", 1))
                        if tt < 7:
                            ctx.wait_now("tensor", "te")
                    ctx.wait_now("gpsimd", "te")
                    ctx.dma("gpsimd",
                            xc_d[:, ci * 128:(ci + 1) * 128].rearrange("(t p) f -> p t f", p=128),
                            xcbuf[:].rearrange("p (t f) -> p t f", f=128), sem="xs")
        for e in Ctx.ENGINES:
            ctx.wait_now(e, "xs")
            ctx.wait_now(e, "sp")
            ctx.wait_now(e, "cs")
            ctx.wait_now(e, "te")
    return dict(b16=b16, c16=c16, Btm=Btm, dt32=dt32, a32=a32, aT=aT, dtT=dtT, ddT=ddT,
                alast=alast)


def emit_P2_dtpost(ctx, g, t):
    vsc, vsc2, dt32, a32 = t["vsc"], t["vsc2"], t["dt32"], t["a32"]
    aT, dtT, ddT, alast, avec = t["aT"], t["dtT"], t["ddT"], t["alast"], t["avec"]
    zeros256, idnf, a_d = t["zeros256"], t["idnf"], t["a_d"]
    A_sb, dtb_sb = g["Alog_sb"], g["dtb_sb"]

    ctx.wait_now("scalar", "ev")
    ctx.op("scalar", lambda s: s.activation(out=avec[0:32, 0:1], in_=A_sb[0:32, 0:1], func=AF.Exp),
           inc=("dts", 1))
    ctx.wait_now("vector", "dts")
    ctx.op("vector", lambda v: v.tensor_scalar(out=avec[0:32, 0:1], in0=avec[0:32, 0:1],
                                               scalar1=-1.0, scalar2=None, op0=ALU.mult))
    ctx.op("vector", lambda v: v.tensor_scalar(out=vsc[:], in0=vsc[:], scalar1=dtb_sb[0:32, 0:1],
                                               scalar2=None, op0=ALU.add), inc=("dtv", 1))
    ctx.wait_now("scalar", "dtv")
    ctx.op("scalar", lambda s: s.activation(out=vsc2[:], in_=vsc[:], func=AF.Abs))
    ctx.op("scalar", lambda s: s.activation(out=vsc2[:], in_=vsc2[:], func=AF.Exp, scale=-1.0))
    ctx.op("scalar", lambda s: s.activation(out=vsc2[:], in_=vsc2[:], func=AF.Ln, bias=1.0),
           inc=("dts", 1))
    ctx.wait_now("vector", "dts")
    ctx.op("vector", lambda v: v.tensor_scalar(out=vsc[:], in0=vsc[:], scalar1=0.0, scalar2=None,
                                               op0=ALU.max))
    ctx.op("vector", lambda v: v.tensor_add(out=dt32[:], in0=vsc[:], in1=vsc2[:]))
    ctx.op("vector", lambda v: v.tensor_scalar(out=vsc[:], in0=dt32[:], scalar1=avec[0:32, 0:1],
                                               scalar2=None, op0=ALU.mult), inc=("dtv", 1))
    ctx.wait_now("vector", "ze")
    for c in range(4):
        ctx.op("vector", lambda v, c=c: v.tensor_tensor_scan(
            out=a32[:, c * 256:(c + 1) * 256], data0=vsc[:, c * 256:(c + 1) * 256],
            data1=zeros256[:], initial=0.0, op0=ALU.add, op1=ALU.add))
    for c in range(4):
        ctx.op("vector", lambda v, c=c: v.tensor_copy(out=alast[:, c:c + 1],
                                                      in_=a32[:, c * 256 + 255:c * 256 + 256]))
    ctx.op("vector", lambda v: v.tensor_copy(out=vsc2[:, 0:4], in_=alast[:, 0:4]), inc=("dtv", 1))
    ctx.wait_now("scalar", "dtv")
    ctx.op("scalar", lambda s: s.activation(out=alast[:, 4:8], in_=vsc2[:, 0:4], func=AF.Exp),
           inc=("dts", 1))
    for c in range(4):
        ctx.op("scalar", lambda s, c=c: s.activation(
            out=vsc[:, c * 256:(c + 1) * 256], in_=a32[:, c * 256:(c + 1) * 256], func=AF.Exp,
            scale=-1.0, bias=alast[:, c:c + 1]), inc=("dts", 1) if c == 3 else None)
    ctx.wait_now("vector", "dts")
    ctx.op("vector", lambda v: v.tensor_mul(out=vsc[:], in0=vsc[:], in1=dt32[:]), inc=("dtv", 1))
    ctx.wait_now("gpsimd", "dtv")
    ctx.dma("gpsimd", a_d[:], a32[:], sem="asp")
    ctx.wait_now("tensor", "dtv")
    pTf0, pTf1 = t["pTf0"], t["pTf1"]
    pTfs = (pTf0, pTf1)
    for i, (src, dst) in enumerate(((a32, aT), (dt32, dtT), (vsc, ddT))):
        for cq in range(8):
            pt = pTfs[cq % 2]
            ctx.op("tensor", lambda te, src=src, cq=cq, pt=pt: te.transpose(
                pt[:, 0:32], src[:, cq * 128:(cq + 1) * 128], idnf[0:32, 0:32]),
                inc=("tp", 1))
            ctx.wait_now("vector", "tp")
            ctx.op("vector", lambda v, dst=dst, cq=cq, pt=pt: v.tensor_copy(
                out=dst[:, cq * 32:(cq + 1) * 32], in_=pt[:, 0:32]),
                inc=("te", 1))
            ctx.wait_now("tensor", "te")


def emit_P2_z(ctx, g, scope):
    nc = ctx.nc
    xnT, wip_d, sigz_d = g["xnT"], g["wip_d"], g["sigz_d"]
    sb = lambda n, s, d: scope.enter_context(nc.sbuf_tensor(n, s, d))
    wz32 = [sb("wz32_0", [128, 512], F32), sb("wz32_1", [128, 512], F32)]
    wz16 = [sb("wz16_0", [128, 512], BF16), sb("wz16_1", [128, 512], BF16)]
    sz16 = sb("sz16", [128, 512], BF16)
    pz = [scope.enter_context(nc.psum_tensor("pz%d" % t, [128, 512], F32)) for t in range(8)]

    cvt_done = [0, 0]
    mm_done = [0, 0]
    for oc in range(8):
        for f in range(16):
            slot = (oc * 16 + f) % 2
            ctx.wait("sync", "zmm", mm_done[slot])
            ctx.dma("sync", wz32[slot][:],
                    wip_d[f * 128:(f + 1) * 128, oc * 512:(oc + 1) * 512], sem="zwl%d" % slot)
            ctx.wait_now("vector", "zwl%d" % slot)
            ctx.wait("vector", "zmm", mm_done[slot])
            ctx.op("vector", lambda v, s=slot: v.tensor_copy(out=wz16[s][:], in_=wz32[s][:]),
                   inc=("zwc", 1))
            cvt_done[slot] = ctx.count("zwc")
            ctx.wait("tensor", "zwc", cvt_done[slot])
            if f == 0:
                ctx.wait_now("tensor", "zev")
            for t in range(8):
                ctx.op("tensor", lambda te, s=slot, f=f, t=t, oc=oc: te.matmul(
                    pz[t][:], xnT[:, f * 1027 + t * 128: f * 1027 + (t + 1) * 128],
                    wz16[s][:], start=(f == 0), stop=(f == 15)),
                    inc=("zmm", 1) if t == 7 else None)
            mm_done[slot] = ctx.count("zmm")
        ctx.wait_now("scalar", "zmm")
        ctx.wait_now("scalar", "zsp")
        for t in range(8):
            ctx.op("scalar", lambda s, t=t: s.activation(out=sz16[:], in_=pz[t][:], func=AF.Sigmoid),
                   inc=("zev", 1))
            ctx.wait_now("gpsimd", "zev")
            ctx.dma("gpsimd", sigz_d[t * 128:(t + 1) * 128, oc * 512:(oc + 1) * 512],
                    sz16[:], sem="zsp")
            if t < 7:
                ctx.wait_now("scalar", "zsp")
    for e in Ctx.ENGINES:
        ctx.wait_now(e, "zsp")


def emit_P2_ssd(ctx, g, scope, sh):
    nc = ctx.nc
    aT, dtT, ddT, Btm, alast = sh["aT"], sh["dtT"], sh["ddT"], sh["Btm"], sh["alast"]
    b16, c16 = sh["b16"], sh["c16"]
    a_d, xc_d, y_d, st_d = g["a_d"], g["xc_d"], g["y_d"], g["st_d"]
    onesf, ssdmask, Dbc, idn = g["onesf_sb"], g["ssdmask_sb"], g["Dbc_sb"], g["idn"]

    sb = lambda n, s, d: scope.enter_context(nc.sbuf_tensor(n, s, d))
    arowrow = sb("arowrow", [1, 8192], F32)
    arow = sb("arow", [128, 8192], F32)
    sT16 = sb("sT16", [128, 512], BF16)
    arg32 = sb("arg32", [128, 2 * 512], F32)
    e32 = sb("e32", [128, 2 * 512], F32)
    wt16 = sb("wt16", [128, 2 * 512], BF16)
    bt16 = sb("bt16", [128, 2 * 128], BF16)
    xct = sb("xct", [128, 2 * 256], BF16)
    yt16 = sb("yt16", [128, 2 * 256], BF16)
    stt32 = sb("stt32", [64, 2 * 128], F32)
    pbr = [scope.enter_context(nc.psum_tensor("pbr%d" % i, [128, 512], F32)) for i in range(2)]
    pscr = [scope.enter_context(nc.psum_tensor("pscr%d" % i, [128, 256], F32)) for i in range(2)]
    pst = [scope.enter_context(nc.psum_tensor("pst%d" % i, [64, 128], F32)) for i in range(2)]
    py = [scope.enter_context(nc.psum_tensor("py%d" % i, [128, 256], F32)) for i in range(2)]

    for e in ("sync", "vector", "scalar", "tensor", "gpsimd"):
        ctx.wait_now(e, "asp")
        ctx.wait_now(e, "xs")
        ctx.wait_now(e, "te")

    hmm_at = {}
    hev_at = {}
    hsp_at = {}
    for c in range(4):
        ctx.wait_now("sync", "bev")
        ctx.dma("sync", arowrow[:].rearrange("a (h t) -> a h t", t=256),
                a_d[:, c * 256:(c + 1) * 256].unsqueeze(0), sem="arl")
        ctx.wait_now("tensor", "arl")
        ctx.wait_now("scalar", "hv")
        for k in range(16):
            if k >= 2:
                ctx.wait("tensor", "bev", ctx.count("bev") - 1)
            ctx.op("tensor", lambda te, k=k: te.matmul(
                pbr[k % 2][:], onesf[0:1, :], arowrow[:, k * 512:(k + 1) * 512],
                start=True, stop=True), inc=("bmm", 1))
            ctx.wait_now("scalar", "bmm")
            ctx.op("scalar", lambda s, k=k: s.copy(out=arow[:, k * 512:(k + 1) * 512],
                                                   in_=pbr[k % 2][:]), inc=("bev", 1))
        ctx.wait_now("tensor", "sev")
        ctx.wait_now("tensor", "hw")
        for jt in range(2):
            ctx.op("tensor", lambda te, c=c, jt=jt: te.matmul(
                pscr[jt][:], b16[:, c * 256 + jt * 128: c * 256 + (jt + 1) * 128],
                c16[:, c * 256:(c + 1) * 256], start=True, stop=True), inc=("smm", 1))
        ctx.wait_now("vector", "smm")
        for jt in range(2):
            ctx.op("vector", lambda v, jt=jt: v.tensor_tensor(
                out=sT16[:, jt * 256:(jt + 1) * 256], in0=pscr[jt][:],
                in1=ssdmask[:, jt * 256:(jt + 1) * 256], op=ALU.mult),
                inc=("sev", 1))

        for h in range(32):
            hg = c * 32 + h
            hb = h % 2
            if hg >= 2:
                ctx.wait("vector", "hmm", hmm_at[hg - 2])
                ctx.wait("vector", "hsp%d" % hb, hsp_at[hg - 2])
            ctx.wait_now("vector", "sev")
            ctx.wait_now("vector", "bev")
            for jt in range(2):
                ctx.op("vector", lambda v, c=c, h=h, hb=hb, jt=jt: v.tensor_scalar(
                    out=arg32[:, hb * 512 + jt * 256: hb * 512 + (jt + 1) * 256],
                    in0=arow[:, h * 256:(h + 1) * 256],
                    scalar1=aT[:, (c * 2 + jt) * 32 + h: (c * 2 + jt) * 32 + h + 1],
                    scalar2=0.0, op0=ALU.subtract, op1=ALU.min), inc=("hv", 1) if jt == 1 else None)
            ctx.wait_now("scalar", "hv")
            ctx.op("scalar", lambda s, hb=hb: s.activation(
                out=e32[:, hb * 512:(hb + 1) * 512], in_=arg32[:, hb * 512:(hb + 1) * 512],
                func=AF.Exp), inc=("hx", 1))
            ctx.wait_now("vector", "hx")
            for jt in range(2):
                ctx.op("vector", lambda v, c=c, h=h, hb=hb, jt=jt: v.scalar_tensor_tensor(
                    out=wt16[:, hb * 512 + jt * 256: hb * 512 + (jt + 1) * 256],
                    in0=e32[:, hb * 512 + jt * 256: hb * 512 + (jt + 1) * 256],
                    scalar=dtT[:, (c * 2 + jt) * 32 + h: (c * 2 + jt) * 32 + h + 1],
                    in1=sT16[:, jt * 256:(jt + 1) * 256], op0=ALU.mult, op1=ALU.mult),
                    inc=("hw", 1) if jt == 1 else None)
            if DBG_SSD and c == 0 and h < 2:
                ctx.wait_now("gpsimd", "hw")
                ctx.dma("gpsimd", g["dbg_wt"][h], wt16[:, hb * 512:(hb + 1) * 512], sem="dbgs")
                ctx.dma("gpsimd", g["dbg_e"][h], e32[:, hb * 512:(hb + 1) * 512], sem="dbgs")
                ctx.dma("gpsimd", g["dbg_st"][h], sT16[:, 0:512], sem="dbgs")
            for qt in range(2):
                ctx.op("vector", lambda v, c=c, h=h, hb=hb, qt=qt: v.tensor_scalar(
                    out=bt16[:, hb * 128 + qt * 64: hb * 128 + (qt + 1) * 64],
                    in0=Btm[:, (c * 2 + qt) * 64:(c * 2 + qt + 1) * 64],
                    scalar1=ddT[:, (c * 2 + qt) * 32 + h: (c * 2 + qt) * 32 + h + 1],
                    scalar2=None, op0=ALU.mult), inc=("hbv", 1) if qt == 1 else None)
            if hg >= 2:
                ctx.wait("sync", "hmm", hmm_at[hg - 2])
            for qt in range(2):
                ctx.dma("sync", xct[:, hb * 256 + qt * 128: hb * 256 + (qt + 1) * 128],
                        xc_d[c * 256 + qt * 128: c * 256 + (qt + 1) * 128, h * 128:(h + 1) * 128],
                        sem="hxc%d" % hb)
            ctx.wait_now("tensor", "hw")
            ctx.wait_now("tensor", "hbv")
            ctx.wait_now("tensor", "hxc%d" % hb)
            if hg >= 2:
                ctx.wait("tensor", "hev", hev_at[hg - 2])
            for qt in range(2):
                ctx.op("tensor", lambda te, hb=hb, qt=qt: te.matmul(
                    pst[hb][0:64, :], bt16[:, hb * 128 + qt * 64: hb * 128 + (qt + 1) * 64],
                    xct[:, hb * 256 + qt * 128: hb * 256 + (qt + 1) * 128],
                    start=(qt == 0), stop=(qt == 1)))
            for qt in range(2):
                for jt in range(2):
                    ctx.op("tensor", lambda te, hb=hb, qt=qt, jt=jt: te.matmul(
                        py[hb][:, qt * 128:(qt + 1) * 128],
                        wt16[:, hb * 512 + jt * 256 + qt * 128: hb * 512 + jt * 256 + (qt + 1) * 128],
                        xct[:, hb * 256 + jt * 128: hb * 256 + (jt + 1) * 128],
                        start=(jt == 0), stop=(jt == 1)),
                        inc=("hmm", 1) if (qt == 1 and jt == 1) else None)
            hmm_at[hg] = ctx.count("hmm")
            ctx.wait_now("vector", "hmm")
            if hg >= 2:
                ctx.wait("vector", "hsp%d" % hb, hsp_at[hg - 2])
            ctx.op("vector", lambda v, hb=hb: v.tensor_copy(
                out=stt32[0:64, hb * 128:(hb + 1) * 128], in_=pst[hb][0:64, :]))
            for qt in range(2):
                ctx.op("vector", lambda v, h=h, hb=hb, qt=qt: v.scalar_tensor_tensor(
                    out=yt16[:, hb * 256 + qt * 128: hb * 256 + (qt + 1) * 128],
                    in0=xct[:, hb * 256 + qt * 128: hb * 256 + (qt + 1) * 128],
                    scalar=Dbc[:, h:h + 1],
                    in1=py[hb][:, qt * 128:(qt + 1) * 128], op0=ALU.mult, op1=ALU.add),
                    inc=("hev", 1) if qt == 1 else None)
            hev_at[hg] = ctx.count("hev")
            ctx.wait_now("gpsimd", "hev")
            ctx.dma("gpsimd", st_d[c, h], stt32[0:64, hb * 128:(hb + 1) * 128], sem="hsp%d" % hb)
            for qt in range(2):
                ctx.dma("gpsimd",
                        y_d[c * 256 + qt * 128: c * 256 + (qt + 1) * 128, h * 128:(h + 1) * 128],
                        yt16[:, hb * 256 + qt * 128: hb * 256 + (qt + 1) * 128], sem="hsp%d" % hb)
            hsp_at[hg] = ctx.count("hsp%d" % hb)
    for e in Ctx.ENGINES:
        ctx.wait_now(e, "hsp0")
        ctx.wait_now(e, "hsp1")


def emit_P2_loc(ctx, g, scope, sh):
    nc = ctx.nc
    alast = sh["alast"]
    st_d, ccA_in = g["st_d"], g["ccA_in"]
    sb = lambda n, s, d: scope.enter_context(nc.sbuf_tensor(n, s, d))
    locacc = sb("locacc", [32, 8192], F32)
    stbuf = sb("stbuf", [32, 8192], F32)

    for c in range(4):
        ctx.wait_now("sync", "lv")
        ctx.dma("sync", stbuf[:], st_d[c].rearrange("h n p -> h (n p)"), sem="lld")
        ctx.wait_now("vector", "lld")
        if c == 0:
            ctx.op("vector", lambda v: v.tensor_copy(out=locacc[:], in_=stbuf[:]), inc=("lv", 1))
        else:
            ctx.op("vector", lambda v, c=c: v.scalar_tensor_tensor(
                out=locacc[:], in0=locacc[:], scalar=alast[:, 4 + c:5 + c], in1=stbuf[:],
                op0=ALU.mult, op1=ALU.add), inc=("lv", 1))
    ctx.wait_now("gpsimd", "lv")
    ctx.dma("gpsimd", ccA_in[0:1, 0:128].rearrange("a (h f) -> a h f", f=4).squeeze(0),
            alast[:, 0:4], sem="lsp")
    ctx.dma("gpsimd", ccA_in[0:1, 128:128 + 32 * 8192].rearrange("a (h f) -> a h f", f=8192).squeeze(0),
            locacc[:], sem="lsp")
    for e in Ctx.ENGINES:
        ctx.wait_now(e, "lsp")



def build_launchA(stages=("p0", "front", "z", "ssd", "loc")):
    nc = bass.Bass("TRN2", target_bir_lowering=False)
    with ExitStack() as es:
        ctx = Ctx(nc, es)
        g = {}
        g["x_loc"] = decl_in(nc, "x_loc", [TOK, D], F32)
        g["x_halo"] = decl_in(nc, "x_halo", [128, D], F32)
        g["wip_d"] = decl_in(nc, "wip", [D, DIP], F32)
        convwt = decl_in(nc, "conv_w_t", [128, 136], F32)
        convbt = decl_in(nc, "conv_b_t", [128, 34], F32)
        alog = decl_in(nc, "A_log", [32, 1], F32)
        dtb = decl_in(nc, "dt_bias", [32, 1], F32)
        dbc = decl_in(nc, "D_bc", [128, 32], F32)
        idn_d = decl_in(nc, "idn128", [128, 128], BF16)
        idnf_d = decl_in(nc, "idn128f", [128, 128], F32)
        onesf_d = decl_in(nc, "ones1x128f", [1, 128], F32)
        ssdm_d = decl_in(nc, "ssdmask", [128, 512], F32)
        g["xq_d"] = decl_out(nc, "xq_d", [TOK, D], BF16)
        g["xc_d"] = decl_out(nc, "xc_d", [TOK, DIN], BF16)
        g["y_d"] = decl_out(nc, "y_d", [TOK, DIN], BF16)
        g["sigz_d"] = decl_out(nc, "sigz_d", [TOK, DIN], BF16)
        g["st_d"] = decl_out(nc, "st_d", [4, 32, 64, 128], F32)
        g["a_d"] = decl_out(nc, "a_d", [32, TOK], F32)
        g["bc_d"] = decl_out(nc, "bc_d", [128, TOK], BF16)
        g["ccA_in"] = decl_out(nc, "locpack", [1, 128 + 32 * 8192], F32)
        if DBG_SSD:
            g["dbg_wt"] = decl_out(nc, "dbg_wt", [2, 128, 512], BF16)
            g["dbg_e"] = decl_out(nc, "dbg_e", [2, 128, 512], F32)
            g["dbg_st"] = decl_out(nc, "dbg_st", [2, 128, 512], BF16)

        g["idn"] = ctx.sb("idn", [128, 128], BF16)
        g["idnf"] = ctx.sb("idnf", [128, 128], F32)
        g["onesf_sb"] = ctx.sb("onesf", [1, 128], F32)
        g["ssdmask_sb"] = ctx.sb("ssdm", [128, 512], F32)
        g["Dbc_sb"] = ctx.sb("dbc", [128, 32], F32)
        g["convw_sb"] = ctx.sb("convw", [128, 136], F32)
        g["convb_sb"] = ctx.sb("convb", [128, 34], F32)
        g["Alog_sb"] = ctx.sb("alog", [32, 1], F32)
        g["dtb_sb"] = ctx.sb("dtbias", [32, 1], F32)
        g["xnT"] = ctx.sb("xnT", [128, 16 * 1027], BF16)
        emit_load_consts(ctx, [
            (g["idn"][:], idn_d), (g["idnf"][:], idnf_d), (g["onesf_sb"][:], onesf_d),
            (g["ssdmask_sb"][:], ssdm_d), (g["Dbc_sb"][:], dbc),
            (g["convw_sb"][:], convwt), (g["convb_sb"][:], convbt),
            (g["Alog_sb"][:], alog), (g["dtb_sb"][:], dtb),
        ])

        with ctx.scope() as s0:
            emit_P0_ln_quant(ctx, g, s0)
        with ctx.scope() as s1:
            if "front" in stages:
                sh = emit_P2_l0attn_front(ctx, g, s1)
            if "z" in stages:
                with ctx.scope() as s2:
                    emit_P2_z(ctx, g, s2)
            if "ssd" in stages:
                with ctx.scope() as s3:
                    emit_P2_ssd(ctx, g, s3, sh)
            if "loc" in stages:
                with ctx.scope() as s4:
                    emit_P2_loc(ctx, g, s4, sh)
        for e in Ctx.ENGINES:
            for s in ("p0sp", "xs", "sp", "zsp", "hsp", "lsp", "asp"):
                ctx.wait_now(e, s)
        ctx.emit()
    return nc


A_INPUTS = ["x_loc", "x_halo", "wip", "conv_w_t", "conv_b_t", "A_log", "dt_bias",
            "D_bc", "idn128", "idn128f", "ones1x128f", "ssdmask"]


def run_launchA(in_maps):
    nc = build_launchA()
    ims = [{k: im[k] for k in A_INPUTS} for im in in_maps]
    res = run_bass_kernel_spmd(nc, ims, core_ids=list(range(8)))
    return res.results


def run_unfused(in_maps):
    resA = run_launchA(in_maps)
    resB = run_launchB(in_maps, resA)
    resC = run_launchC(in_maps, resB)
    resD = run_launchD(in_maps, resB, resC)
    out = np.zeros((B, L, D), np.float32)
    for k in range(8):
        b, j = k // 4, k % 4
        out[b, j * TOK:(j + 1) * TOK] = np.asarray(resD[k]["out_loc"], np.float32)
    return out


def kernel(**inputs):
    in_maps = host_prep(inputs)
    return run_unfused(in_maps)


def build_launchA_p0dbg():
    nc = bass.Bass("TRN2", target_bir_lowering=False)
    with ExitStack() as es:
        ctx = Ctx(nc, es)
        g = {}
        g["x_loc"] = decl_in(nc, "x_loc", [TOK, D], F32)
        g["x_halo"] = decl_in(nc, "x_halo", [128, D], F32)
        idn_d = decl_in(nc, "idn128", [128, 128], BF16)
        g["xq_d"] = decl_out(nc, "xq_d", [TOK, D], BF16)
        xnT_d = decl_out(nc, "xnT_d", [128, 16 * 1027], BF16)
        g["idn"] = ctx.sb("idn", [128, 128], BF16)
        g["xnT"] = ctx.sb("xnT", [128, 16 * 1027], BF16)
        emit_load_consts(ctx, [(g["idn"][:], idn_d)])
        with ctx.scope() as s0:
            emit_P0_ln_quant(ctx, g, s0)
        ctx.dma("gpsimd", xnT_d, g["xnT"][:], sem="dbg")
        for e in Ctx.ENGINES:
            ctx.wait_now(e, "dbg")
            ctx.wait_now(e, "p0sp")
        ctx.emit()
    return nc


def emit_T_transpose_in(ctx, g, scope, src_d, dst_sb, nt, name):
    nc = ctx.nc
    idn = g["idn"]
    sb = lambda n, s, d: scope.enter_context(nc.sbuf_tensor(n, s, d))
    tin = sb(name + "_tin", [128, 2048], BF16)
    pq0 = scope.enter_context(nc.psum_tensor(name + "_pq0", [128, 1024], BF16))
    pq1 = scope.enter_context(nc.psum_tensor(name + "_pq1", [128, 1024], BF16))
    pqs = (pq0, pq1)
    ntok = nt * 128
    for t in range(nt):
        ctx.wait_now("sync", name + "v")
        ctx.dma("sync", tin[:], src_d[t * 128:(t + 1) * 128, :], sem=name + "l")
        ctx.wait_now("tensor", name + "l")
        ctx.wait_now("tensor", name + "v")
        for f in range(16):
            pt = pqs[f % 2]
            ctx.op("tensor", lambda te, f=f, pt=pt: te.transpose(
                pt[:, 0:128], tin[:, f * 128:(f + 1) * 128], idn[:]), inc=(name + "t", 1))
            ctx.wait_now("vector", name + "t")
            ctx.op("vector", lambda v, f=f, t=t, pt=pt: v.tensor_copy(
                out=dst_sb[:, f * ntok + t * 128: f * ntok + (t + 1) * 128],
                in_=pt[:, 0:128]), inc=(name + "v", 1))
            if f < 15:
                ctx.wait_now("tensor", name + "v")
    for e in Ctx.ENGINES:
        ctx.wait_now(e, name + "v")


def emit_P3_ffn0(ctx, g, scope):
    nc = ctx.nc
    xqT = g["xqT"]
    wg0_d, wu0_d, wd0_d = g["wg0_d"], g["wu0_d"], g["wd0_d"]
    p_d = g["p_d"]
    g0 = g["g0_sb"]

    sA = ctx.scope()
    sb = lambda n, s, d: sA.enter_context(nc.sbuf_tensor(n, s, d))
    wbg = [sb("f0wg_%d" % i, [128, 2048], F32) for i in range(2)]
    wbu = [sb("f0wu_%d" % i, [128, 2048], F32) for i in range(2)]
    wsg = [sb("f0sg_%d" % i, [128, 2048], BF16) for i in range(2)]
    wsu = [sb("f0su_%d" % i, [128, 2048], BF16) for i in range(2)]
    silug = sb("f0_silug", [128, 1024], F32)
    ptile = sb("f0_p", [128, 1024], BF16)
    pg = [sA.enter_context(nc.psum_tensor("f0pg%d" % t, [128, 512], F32)) for t in range(2)]
    pu = [sA.enter_context(nc.psum_tensor("f0pu%d" % t, [128, 512], F32)) for t in range(2)]

    mm_done = [0, 0]
    for ot in range(64):
        slot = ot % 2
        ctx.wait("sync", "f0m", mm_done[slot])
        ctx.dma("sync", wbg[slot][:].rearrange("p (c o) -> p c o", o=128),
                wg0_d[:, ot * 128:(ot + 1) * 128].rearrange("(c p) o -> p c o", p=128), sem="f0l%d" % slot)
        ctx.dma("sync", wbu[slot][:].rearrange("p (c o) -> p c o", o=128),
                wu0_d[:, ot * 128:(ot + 1) * 128].rearrange("(c p) o -> p c o", p=128), sem="f0l%d" % slot)
        ctx.wait_now("vector", "f0l%d" % slot)
        ctx.wait("vector", "f0m", mm_done[slot])
        ctx.op("vector", lambda v, s=slot: v.tensor_scalar(
            out=wsg[s][:], in0=wbg[s][:], scalar1=0.0, scalar2=0.5,
            op0=ALU.is_ge, op1=ALU.subtract))
        ctx.op("vector", lambda v, s=slot: v.tensor_scalar(
            out=wsu[s][:], in0=wbu[s][:], scalar1=0.0, scalar2=0.5,
            op0=ALU.is_ge, op1=ALU.subtract), inc=("f0c", 1))
        ctx.wait_now("tensor", "f0c")
        ctx.wait_now("tensor", "f0e")
        for tg in range(2):
            for f in range(16):
                ctx.op("tensor", lambda te, s=slot, f=f, tg=tg: te.matmul(
                    pg[tg][:], wsg[s][:, f * 128:(f + 1) * 128],
                    xqT[:, f * 1024 + tg * 512: f * 1024 + (tg + 1) * 512],
                    start=(f == 0), stop=(f == 15)), inc=None)
            for f in range(16):
                ctx.op("tensor", lambda te, s=slot, f=f, tg=tg: te.matmul(
                    pu[tg][:], wsu[s][:, f * 128:(f + 1) * 128],
                    xqT[:, f * 1024 + tg * 512: f * 1024 + (tg + 1) * 512],
                    start=(f == 0), stop=(f == 15)),
                    inc=("f0m", 1) if (tg == 1 and f == 15) else None)
        mm_done[slot] = ctx.count("f0m")
        ctx.wait_now("scalar", "f0m")
        for tg in range(2):
            ctx.op("scalar", lambda s, tg=tg: s.activation(
                out=silug[:, tg * 512:(tg + 1) * 512], in_=pg[tg][:], func=AF.Silu, scale=2.0),
                inc=("f0sg", 1))
        ctx.wait_now("vector", "f0sg")
        ctx.wait_now("vector", "f0sp")
        for tg in range(2):
            ctx.op("vector", lambda v, tg=tg: v.scalar_tensor_tensor(
                out=ptile[:, tg * 512:(tg + 1) * 512], in0=pu[tg][:], scalar=2.0,
                in1=silug[:, tg * 512:(tg + 1) * 512], op0=ALU.mult, op1=ALU.mult),
                inc=("f0e", 1))
        ctx.wait_now("gpsimd", "f0e")
        ctx.dma("gpsimd", p_d[ot * 128:(ot + 1) * 128, :], ptile[:], sem="f0sp")
    for e in Ctx.ENGINES:
        ctx.wait_now(e, "f0sp")
    sA.close()

    with ctx.scope() as s2:
        sb2 = lambda n, s, d: s2.enter_context(nc.sbuf_tensor(n, s, d))
        dw = [sb2("f0dw%d" % i, [128, 512], F32) for i in range(2)]
        ds = [sb2("f0ds%d" % i, [128, 512], BF16) for i in range(2)]
        ptl = [sb2("f0ptl%d" % i, [128, 1024], BF16) for i in range(2)]
        pd = [s2.enter_context(nc.psum_tensor("f0pd%d" % t, [128, 512], F32)) for t in range(8)]
        mmd = [0, 0]
        for og in range(4):
            for i in range(64):
                slot = i % 2
                ctx.wait("sync", "d0m", mmd[slot])
                ctx.dma("sync", dw[slot][:], wd0_d[i * 128:(i + 1) * 128, og * 512:(og + 1) * 512],
                        sem="d0l%d" % slot)
                ctx.dma("sync", ptl[slot][:], p_d[i * 128:(i + 1) * 128, :], sem="d0l%d" % slot)
                ctx.wait_now("vector", "d0l%d" % slot)
                ctx.wait("vector", "d0m", mmd[slot])
                ctx.op("vector", lambda v, s=slot: v.tensor_scalar(
                    out=ds[s][:], in0=dw[s][:], scalar1=0.0, scalar2=0.5,
                    op0=ALU.is_ge, op1=ALU.subtract), inc=("d0c", 1))
                ctx.wait_now("tensor", "d0c")
                if i == 0:
                    ctx.wait_now("tensor", "d0e")
                for t in range(8):
                    ctx.op("tensor", lambda te, s=slot, t=t, i=i: te.matmul(
                        pd[t][:], ptl[s][:, t * 128:(t + 1) * 128], ds[s][:],
                        start=(i == 0), stop=(i == 63)),
                        inc=("d0m", 1) if t == 7 else None)
                mmd[slot] = ctx.count("d0m")
            ctx.wait_now("scalar", "d0m")
            for t in range(8):
                ctx.op("scalar", lambda s, t=t, og=og: s.activation(
                    out=g0[:, t * 2048 + og * 512: t * 2048 + (og + 1) * 512],
                    in_=pd[t][:], func=AF.Silu, scale=2.0), inc=("d0e", 1))
    for e in Ctx.ENGINES:
        ctx.wait_now(e, "d0e")


def emit_P4_ssdB(ctx, g, scope):
    nc = ctx.nc
    locall = g["locall_d"]
    st_d, a_d, bc_d, y_d, sigz_d, xc_d = g["st_d"], g["a_d"], g["bc_d"], g["y_d"], g["sigz_d"], g["xc_d"]
    prev_d = g["prev_d"]
    Mbig, incbig = g["Mbig_sb"], g["incbig_sb"]
    normw = g["normw_sb"]
    wo_d = g["wo_d"]
    g0 = g["g0_sb"]
    x_loc = g["x_loc"]
    x2_d = g["x2_d"]
    xn2T_d = g["xn2T_d"]
    idn, idnf = g["idn"], g["idnf"]

    sb = lambda n, s, d: scope.enter_context(nc.sbuf_tensor(n, s, d))
    with ctx.scope() as s1:
        sb1 = lambda n, s, d: s1.enter_context(nc.sbuf_tensor(n, s, d))
        acc = sb1("p4_acc", [32, 8192], F32)
        stb = sb1("p4_stb", [32, 8192], F32)
        atot = sb1("p4_atot", [32, 8], F32)
        dmat = sb1("p4_dmat", [32, 8], F32)
        tmp88 = sb1("p4_tmp88", [32, 64], F32)
        a32 = sb1("p4_a32", [32, 1024], F32)
        ecol = sb1("p4_ecol", [32, 4], F32)
        for m in range(8):
            ctx.dma("sync", stb[:, 0:4], locall[m:m + 1, 0:128].rearrange(
                "a (h f) -> a h f", f=4).squeeze(0), sem="p4l")
            ctx.wait_now("vector", "p4l")
            ctx.op("vector", lambda v, m=m: v.reduce_sum(out=atot[:, m:m + 1], in_=stb[:, 0:4],
                                                         axis=AX.X), inc=("p4v", 1))
            ctx.wait_now("sync", "p4v")
        ctx.op("vector", lambda v: v.tensor_tensor(
            out=tmp88[:].rearrange("p (j m) -> p j m", m=8),
            in0=Mbig[:].rearrange("p (j m) -> p j m", m=8),
            in1=atot[:].unsqueeze(1).broadcast_to([32, 8, 8]), op=ALU.mult))
        ctx.op("vector", lambda v: v.reduce_sum(out=dmat[:], in_=tmp88[:].rearrange(
            "p (j m) -> p j m", m=8), axis=AX.X), inc=("p4v", 1))
        ctx.wait_now("scalar", "p4v")
        ctx.op("scalar", lambda s: s.activation(out=dmat[:], in_=dmat[:], func=AF.Exp),
               inc=("p4s", 1))
        ctx.wait_now("vector", "p4s")
        ctx.op("vector", lambda v: v.tensor_mul(out=dmat[:], in0=dmat[:], in1=incbig[:]),
               inc=("p4v", 1))
        ctx.op("vector", lambda v: v.memset(acc[:], 0.0), inc=("p4v", 1))
        for m in range(8):
            ctx.wait_now("sync", "p4v")
            ctx.dma("sync", stb[:], locall[m:m + 1, 128:128 + 32 * 8192].rearrange(
                "a (h f) -> a h f", f=8192).squeeze(0), sem="p4l")
            ctx.wait_now("vector", "p4l")
            ctx.op("vector", lambda v, m=m: v.scalar_tensor_tensor(
                out=acc[:], in0=stb[:], scalar=dmat[:, m:m + 1], in1=acc[:],
                op0=ALU.mult, op1=ALU.add), inc=("p4v", 1))
        ctx.dma("sync", a32[:], a_d[:], sem="p4l")
        ctx.wait_now("vector", "p4l")
        for c in range(4):
            ctx.op("vector", lambda v, c=c: v.tensor_copy(
                out=ecol[:, c:c + 1], in_=a32[:, c * 256 + 255:c * 256 + 256]))
        ctx.op("vector", lambda v: v.tensor_copy(out=stb[:, 0:4], in_=ecol[:, 0:4]), inc=("p4v", 1))
        ctx.wait_now("scalar", "p4v")
        ctx.op("scalar", lambda s: s.activation(out=ecol[:, 0:4], in_=stb[:, 0:4], func=AF.Exp),
               inc=("p4s", 1))
        ctx.wait_now("vector", "p4s")
        ctx.wait_now("gpsimd", "p4v")
        ctx.dma("gpsimd", prev_d[0].rearrange("h n p -> h (n p)"), acc[:], sem="p4sp")
        for c in range(1, 4):
            ctx.wait_now("sync", "p4v")
            ctx.dma("sync", stb[:], st_d[c - 1].rearrange("h n p -> h (n p)"), sem="p4l")
            ctx.wait_now("vector", "p4l")
            ctx.wait_now("vector", "p4sp")
            ctx.op("vector", lambda v, c=c: v.scalar_tensor_tensor(
                out=acc[:], in0=acc[:], scalar=ecol[:, c - 1:c], in1=stb[:],
                op0=ALU.mult, op1=ALU.add), inc=("p4v", 1))
            ctx.wait_now("gpsimd", "p4v")
            ctx.dma("gpsimd", prev_d[c].rearrange("h n p -> h (n p)"), acc[:], sem="p4sp")
        for e in Ctx.ENGINES:
            ctx.wait_now(e, "p4sp")

    ynT = scope.enter_context(nc.sbuf_tensor("ynT", [128, 32 * 1024], BF16))
    with ctx.scope() as s2:
        sb2 = lambda n, s, d: s2.enter_context(nc.sbuf_tensor(n, s, d))
        c16 = sb2("p4_c16", [64, 1024], BF16)
        a32b = sb2("p4_a32b", [32, 1024], F32)
        aT = sb2("p4_aT", [128, 64], F32)
        eaT = sb2("p4_eaT", [128, 64], F32)
        prevT = sb2("p4_prevT", [64, 2 * 128], F32)
        prevT16 = sb2("p4_prevT16", [64, 2 * 128], BF16)
        ytile = sb2("p4_yt", [128, 4096], BF16)
        sgt = sb2("p4_sg", [128, 4096], BF16)
        xct = sb2("p4_xc", [128, 4096], BF16)
        ygate = sb2("p4_ygate", [128, 4096], F32)
        sqscr = sb2("p4_sqscr", [128, 128], F32)
        rstd = sb2("p4_rstd", [128, 32], F32)
        yn16 = sb2("p4_yn16", [128, 4096], BF16)
        t1 = sb2("p4_t1", [128, 128], F32)
        pyi = [s2.enter_context(nc.psum_tensor("p4pyi%d" % i, [128, 128], F32)) for i in range(2)]
        pTq = [s2.enter_context(nc.psum_tensor("p4pt%d" % i, [128, 1024], BF16)) for i in range(2)]
        pTf = [s2.enter_context(nc.psum_tensor("p4ptf%d" % i, [128, 512], F32)) for i in range(2)]

        ctx.dma("sync", c16[:], bc_d[64:128, :], sem="p4lc")
        ctx.dma("sync", a32b[:], a_d[:], sem="p4lc")
        for e in ("vector", "scalar", "tensor"):
            ctx.wait_now(e, "p4lc")
        for c in range(4):
            ctx.wait_now("tensor", "yev")
            for cq in range(2):
                ctx.op("tensor", lambda te, c=c, cq=cq: te.transpose(
                    pTf[cq][:, 0:32], a32b[:, (c * 2 + cq) * 128:(c * 2 + cq + 1) * 128],
                    idnf[0:32, 0:32]), inc=("ytp", 1))
                ctx.wait_now("vector", "ytp")
                ctx.op("vector", lambda v, cq=cq: v.tensor_copy(
                    out=aT[:, cq * 32:(cq + 1) * 32], in_=pTf[cq][:, 0:32]), inc=("yv", 1))
            ctx.wait_now("scalar", "yv")
            ctx.op("scalar", lambda s: s.activation(out=eaT[:], in_=aT[:], func=AF.Exp),
                   inc=("ys", 1))
            for qt in range(2):
                tok0 = c * 256 + qt * 128
                ctx.wait_now("sync", "yev")
                ctx.dma("sync", ytile[:], y_d[tok0:tok0 + 128, :], sem="p4l")
                ctx.dma("sync", sgt[:], sigz_d[tok0:tok0 + 128, :], sem="p4l")
                ctx.dma("sync", xct[:], xc_d[tok0:tok0 + 128, :], sem="p4l")
                ctx.wait_now("vector", "p4l")
                ctx.wait_now("vector", "ys")
                for h in range(32):
                    hb = h % 2
                    ctx.wait_now("gpsimd", "ymm")
                    ctx.dma("gpsimd", prevT[:, hb * 128:(hb + 1) * 128],
                            prev_d[c, h], sem="ypl%d" % hb)
                    ctx.wait_now("vector", "ypl%d" % hb)
                    ctx.op("vector", lambda v, hb=hb: v.tensor_copy(
                        out=prevT16[:, hb * 128:(hb + 1) * 128],
                        in_=prevT[:, hb * 128:(hb + 1) * 128]), inc=("ypc", 1))
                    ctx.wait_now("tensor", "ypc")
                    ctx.wait_now("tensor", "yevh")
                    ctx.op("tensor", lambda te, c=c, qt=qt, h=h, hb=hb: te.matmul(
                        pyi[hb][:], c16[:, c * 256 + qt * 128: c * 256 + (qt + 1) * 128],
                        prevT16[:, hb * 128:(hb + 1) * 128], start=True, stop=True),
                        inc=("ymm", 1))
                    ctx.wait_now("vector", "ymm")
                    ctx.op("vector", lambda v, qt=qt, h=h, hb=hb: v.tensor_scalar(
                        out=t1[:], in0=pyi[hb][:], scalar1=eaT[:, qt * 32 + h: qt * 32 + h + 1],
                        scalar2=None, op0=ALU.mult), inc=("yevh", 1))
                    ctx.op("vector", lambda v, h=h: v.tensor_tensor(
                        out=t1[:], in0=t1[:], in1=ytile[:, h * 128:(h + 1) * 128], op=ALU.add))
                    ctx.op("vector", lambda v, h=h: v.tensor_tensor(
                        out=ygate[:, h * 128:(h + 1) * 128], in0=t1[:],
                        in1=sgt[:, h * 128:(h + 1) * 128], op=ALU.mult), inc=("yv", 1))
                ctx.wait_now("scalar", "yv")
                for h in range(32):
                    ctx.op("scalar", lambda s, h=h: s.activation(
                        out=sqscr[:], in_=ygate[:, h * 128:(h + 1) * 128], func=AF.Square,
                        accum_out=rstd[:, h:h + 1]), inc=("ys", 1) if h == 31 else None)
                ctx.wait_now("vector", "ys")
                ctx.op("vector", lambda v: v.tensor_scalar(
                    out=rstd[:], in0=rstd[:], scalar1=1.0 / 128, scalar2=1e-6,
                    op0=ALU.mult, op1=ALU.add), inc=("yv", 1))
                ctx.wait_now("scalar", "yv")
                ctx.op("scalar", lambda s: s.activation(out=rstd[:], in_=rstd[:], func=AF.Sqrt),
                       inc=("ys", 1))
                ctx.wait_now("vector", "ys")
                ctx.op("vector", lambda v: v.reciprocal(out=rstd[:], in_=rstd[:]), inc=("yv", 1))
                for h in range(32):
                    ctx.op("vector", lambda v, h=h: v.scalar_tensor_tensor(
                        out=yn16[:, h * 128:(h + 1) * 128], in0=ygate[:, h * 128:(h + 1) * 128],
                        scalar=rstd[:, h:h + 1], in1=normw[:, h * 128:(h + 1) * 128],
                        op0=ALU.mult, op1=ALU.mult), inc=("yv", 1) if h == 31 else None)
                ctx.wait_now("tensor", "yv")
                for h in range(32):
                    pq = pTq[h % 2]
                    ctx.op("tensor", lambda te, h=h, pq=pq: te.transpose(
                        pq[:, 0:128], yn16[:, h * 128:(h + 1) * 128], idn[:]), inc=("ytp", 1))
                    ctx.wait_now("vector", "ytp")
                    ctx.op("vector", lambda v, h=h, pq=pq, tok0=tok0: v.tensor_copy(
                        out=ynT[:, h * 1024 + tok0: h * 1024 + tok0 + 128], in_=pq[:, 0:128]),
                        inc=("yev", 1))
                    if h < 31:
                        ctx.wait_now("tensor", "yev")
        for e in Ctx.ENGINES:
            ctx.wait_now(e, "yev")

    if DBG_B:
        ctx.dma("gpsimd", g["dbg_ynT"], ynT[:], sem="dbgb")
        ctx.dma("gpsimd", g["dbg_g0"], g["g0_sb"][:], sem="dbgb")
        for e in Ctx.ENGINES:
            ctx.wait_now(e, "dbgb")
    with ctx.scope() as s3:
        sb3 = lambda n, s, d: s3.enter_context(nc.sbuf_tensor(n, s, d))
        wob = [sb3("p4wo%d" % i, [128, 512], F32) for i in range(2)]
        wob16 = [sb3("p4wo16%d" % i, [128, 512], BF16) for i in range(2)]
        xtl = sb3("p4_xtl", [128, 512], F32)
        x2t = sb3("p4_x2t", [128, 512], F32)
        po = [s3.enter_context(nc.psum_tensor("p4po%d" % t, [128, 512], F32)) for t in range(8)]
        mmd = [0, 0]
        for og in range(4):
            for f in range(32):
                slot = f % 2
                ctx.wait("sync", "om", mmd[slot])
                ctx.dma("sync", wob[slot][:], wo_d[f * 128:(f + 1) * 128, og * 512:(og + 1) * 512],
                        sem="ol%d" % slot)
                ctx.wait_now("vector", "ol%d" % slot)
                ctx.wait("vector", "om", mmd[slot])
                ctx.op("vector", lambda v, s=slot: v.tensor_copy(out=wob16[s][:], in_=wob[s][:]),
                       inc=("oc", 1))
                ctx.wait_now("tensor", "oc")
                if f == 0:
                    ctx.wait_now("tensor", "oev")
                for t in range(8):
                    ctx.op("tensor", lambda te, s=slot, f=f, t=t: te.matmul(
                        po[t][:], ynT[:, f * 1024 + t * 128: f * 1024 + (t + 1) * 128],
                        wob16[s][:], start=(f == 0), stop=(f == 31)),
                        inc=("om", 1) if t == 7 else None)
                mmd[slot] = ctx.count("om")
            for t in range(8):
                ctx.wait_now("sync", "oev")
                ctx.dma("sync", xtl[:], x_loc[t * 128:(t + 1) * 128, og * 512:(og + 1) * 512],
                        sem="oxl")
                ctx.wait_now("vector", "oxl")
                ctx.wait_now("vector", "om")
                ctx.wait_now("vector", "osp")
                ctx.op("vector", lambda v, t=t, og=og: v.tensor_tensor(
                    out=x2t[:], in0=po[t][:], in1=g0[:, t * 2048 + og * 512: t * 2048 + (og + 1) * 512],
                    op=ALU.mult))
                ctx.op("vector", lambda v: v.tensor_add(out=x2t[:], in0=x2t[:], in1=xtl[:]),
                       inc=("oev", 1))
                ctx.wait_now("gpsimd", "oev")
                ctx.dma("gpsimd", x2_d[t * 128:(t + 1) * 128, og * 512:(og + 1) * 512],
                        x2t[:], sem="osp")
        for e in Ctx.ENGINES:
            ctx.wait_now(e, "osp")

    with ctx.scope() as s4:
        sb4 = lambda n, s, d: s4.enter_context(nc.sbuf_tensor(n, s, d))
        xn2T = s4.enter_context(nc.sbuf_tensor("xn2T", [128, 16 * 1024], BF16))
        g["xn2T_sb"] = xn2T
        xt = sb4("p4_x2", [128, 2048], F32)
        st = sb4("p4_st", [128, 4], F32)
        sq = sb4("p4_sq", [128, 2048], F32)
        xn16 = sb4("p4_xn16", [128, 2048], BF16)
        pq = [s4.enter_context(nc.psum_tensor("p4lq%d" % i, [128, 1024], BF16)) for i in range(2)]
        for t in range(8):
            ctx.wait_now("sync", "l2v")
            ctx.wait_now("sync", "l2s")
            ctx.dma("sync", xt[:], x2_d[t * 128:(t + 1) * 128, :], sem="l2l")
            ctx.wait_now("vector", "l2l")
            ctx.op("vector", lambda v: v.reduce_sum(out=st[:, 0:1], in_=xt[:], axis=AX.X),
                   inc=("l2v", 1))
            ctx.wait_now("scalar", "l2l")
            ctx.op("scalar", lambda s: s.activation(out=sq[:], in_=xt[:], func=AF.Square,
                                                    accum_out=st[:, 1:2]), inc=("l2s", 1))
            ctx.wait_now("vector", "l2s")
            ctx.op("vector", lambda v: v.tensor_scalar(out=st[:, 0:1], in0=st[:, 0:1],
                                                       scalar1=1.0 / 2048, scalar2=None, op0=ALU.mult))
            ctx.op("vector", lambda v: v.tensor_scalar(out=st[:, 2:3], in0=st[:, 1:2],
                                                       scalar1=1.0 / 2048, scalar2=None, op0=ALU.mult))
            ctx.op("vector", lambda v: v.tensor_tensor(out=st[:, 3:4], in0=st[:, 0:1],
                                                       in1=st[:, 0:1], op=ALU.mult))
            ctx.op("vector", lambda v: v.tensor_scalar(out=st[:, 2:3], in0=st[:, 2:3],
                                                       scalar1=st[:, 3:4], scalar2=1e-5,
                                                       op0=ALU.subtract, op1=ALU.add), inc=("l2v", 1))
            ctx.wait_now("scalar", "l2v")
            ctx.op("scalar", lambda s: s.sqrt(out=st[:, 2:3], in_=st[:, 2:3]), inc=("l2s", 1))
            ctx.wait_now("vector", "l2s")
            ctx.op("vector", lambda v: v.reciprocal(out=st[:, 2:3], in_=st[:, 2:3]))
            ctx.op("vector", lambda v: v.tensor_scalar(out=xn16[:], in0=xt[:],
                                                       scalar1=st[:, 0:1], scalar2=st[:, 2:3],
                                                       op0=ALU.subtract, op1=ALU.mult), inc=("l2v", 1))
            ctx.wait_now("tensor", "l2v")
            for f in range(16):
                pt = pq[f % 2]
                ctx.op("tensor", lambda te, f=f, pt=pt: te.transpose(
                    pt[:, 0:128], xn16[:, f * 128:(f + 1) * 128], idn[:]), inc=("l2t", 1))
                ctx.wait_now("vector", "l2t")
                ctx.op("vector", lambda v, f=f, t=t, pt=pt: v.tensor_copy(
                    out=xn2T[:, f * 1024 + t * 128: f * 1024 + (t + 1) * 128],
                    in_=pt[:, 0:128]), inc=("l2v", 1))
                if f < 15:
                    ctx.wait_now("tensor", "l2v")
        ctx.wait_now("gpsimd", "l2v")
        for f in range(16):
            ctx.dma("gpsimd", xn2T_d[f * 128:(f + 1) * 128, :],
                    xn2T[:, f * 1024:(f + 1) * 1024], sem="l2sp")
        for e in Ctx.ENGINES:
            ctx.wait_now(e, "l2sp")


B_INPUTS = ["x_loc", "xq_d", "y_d", "sigz_d", "xc_d", "st_d", "a_d", "bc_d",
            "locall", "Mbig", "incbig", "wg0", "wu0", "wd0", "wo", "normw_bc",
            "idn128", "idn128f"]


def build_launchB():
    nc = bass.Bass("TRN2", target_bir_lowering=False)
    with ExitStack() as es:
        ctx = Ctx(nc, es)
        g = {}
        g["x_loc"] = decl_in(nc, "x_loc", [TOK, D], F32)
        xq_d = decl_in(nc, "xq_d", [TOK, D], BF16)
        g["y_d"] = decl_in(nc, "y_d", [TOK, DIN], BF16)
        g["sigz_d"] = decl_in(nc, "sigz_d", [TOK, DIN], BF16)
        g["xc_d"] = decl_in(nc, "xc_d", [TOK, DIN], BF16)
        g["st_d"] = decl_in(nc, "st_d", [4, 32, 64, 128], F32)
        g["a_d"] = decl_in(nc, "a_d", [32, TOK], F32)
        g["bc_d"] = decl_in(nc, "bc_d", [128, TOK], BF16)
        g["locall_d"] = decl_in(nc, "locall", [8, 128 + 32 * 8192], F32)
        mbig_d = decl_in(nc, "Mbig", [32, 64], F32)
        incbig_d = decl_in(nc, "incbig", [32, 8], F32)
        g["wg0_d"] = decl_in(nc, "wg0", [D, FF0], F32)
        g["wu0_d"] = decl_in(nc, "wu0", [D, FF0], F32)
        g["wd0_d"] = decl_in(nc, "wd0", [FF0, D], F32)
        g["wo_d"] = decl_in(nc, "wo", [DIN, D], F32)
        normw_d = decl_in(nc, "normw_bc", [128, DIN], F32)
        idn_d = decl_in(nc, "idn128", [128, 128], BF16)
        idnf_d = decl_in(nc, "idn128f", [128, 128], F32)
        g["x2_d"] = decl_out(nc, "x2_d", [TOK, D], F32)
        g["xn2T_d"] = decl_out(nc, "xn2T_d", [D, TOK], BF16)
        g["p_d"] = decl_scratch(nc, "p_d", [FF0, TOK], BF16, False)
        g["prev_d"] = decl_scratch(nc, "prev_d", [4, 32, 64, 128], F32, True)
        if DBG_B:
            g["dbg_g0"] = decl_out(nc, "dbg_g0", [128, 8 * 2048], BF16)
            g["dbg_ynT"] = decl_out(nc, "dbg_ynT", [128, 32 * 1024], BF16)

        g["idn"] = ctx.sb("idn", [128, 128], BF16)
        g["idnf"] = ctx.sb("idnf", [128, 128], F32)
        g["Mbig_sb"] = ctx.sb("mbig_sb", [32, 64], F32)
        g["incbig_sb"] = ctx.sb("incbig_sb", [32, 8], F32)
        g["normw_sb"] = ctx.sb("normw_sb", [128, DIN], F32)
        g["g0_sb"] = ctx.sb("g0", [128, 8 * 2048], BF16)
        emit_load_consts(ctx, [
            (g["idn"][:], idn_d), (g["idnf"][:], idnf_d),
            (g["Mbig_sb"][:], mbig_d), (g["incbig_sb"][:], incbig_d),
            (g["normw_sb"][:], normw_d),
        ])
        with ctx.scope() as s0:
            g["xqT"] = s0.enter_context(nc.sbuf_tensor("xqT", [128, 16 * 1024], BF16))
            with ctx.scope() as s0a:
                emit_T_transpose_in(ctx, g, s0a, xq_d, g["xqT"], 8, "xqt")
            with ctx.scope() as s0b:
                emit_P3_ffn0(ctx, g, s0b)
        with ctx.scope() as s1:
            emit_P4_ssdB(ctx, g, s1)
        ctx.emit()
    return nc


def run_launchB(in_maps, resA):
    nc = build_launchB()
    locall = np.stack([np.asarray(r["locpack"][0], np.float32) for r in resA])
    ims = []
    for k in range(8):
        im = {n: in_maps[k][n] for n in ("x_loc", "Mbig", "incbig", "wg0", "wu0", "wd0",
                                         "wo", "normw_bc", "idn128", "idn128f")}
        for n in ("xq_d", "y_d", "sigz_d", "xc_d", "st_d", "a_d", "bc_d"):
            im[n] = np.asarray(resA[k][n])
        im["locall"] = locall
        ims.append(im)
    res = run_bass_kernel_spmd(nc, ims, core_ids=list(range(8)))
    return res.results


def emit_P5_ffn1(ctx, g, scope):
    nc = ctx.nc
    x2T = g["x2T_sb"]
    gw_d, uw_d, dw_d = g["gate_w_d"], g["up_w_d"], g["down_w_d"]
    p1_d, g1_d = g["p1_d"], g["g1_d"]

    gates = scope.enter_context(nc.sbuf_tensor("gates_sb", [128, 44], F32))
    ups = scope.enter_context(nc.sbuf_tensor("ups_sb", [128, 44], F32))
    downs = scope.enter_context(nc.sbuf_tensor("downs_sb", [128, D], F32))
    emit_load_consts(ctx, [(gates[:], g["gates_d"]), (ups[:], g["ups_d"]),
                           (downs[:], g["downs_d"])])
    sA = ctx.scope()
    sb = lambda n, s, d: sA.enter_context(nc.sbuf_tensor(n, s, d))
    wbg = [sb("f1wg_%d" % i, [128, 2048], F32) for i in range(2)]
    wbu = [sb("f1wu_%d" % i, [128, 2048], F32) for i in range(2)]
    wsg = [sb("f1sg_%d" % i, [128, 2048], BF16) for i in range(2)]
    wsu = [sb("f1su_%d" % i, [128, 2048], BF16) for i in range(2)]
    silug = sb("f1_silug", [128, 1024], F32)
    ptile = sb("f1_p", [128, 1024], BF16)
    pg = [sA.enter_context(nc.psum_tensor("f1pg%d" % t, [128, 512], F32)) for t in range(2)]
    pu = [sA.enter_context(nc.psum_tensor("f1pu%d" % t, [128, 512], F32)) for t in range(2)]

    mm_done = [0, 0]
    for ot in range(44):
        slot = ot % 2
        ctx.wait("sync", "f1m", mm_done[slot])
        ctx.dma("sync", wbg[slot][:].rearrange("p (c o) -> p c o", o=128),
                gw_d[:, ot * 128:(ot + 1) * 128].rearrange("(c p) o -> p c o", p=128), sem="f1l%d" % slot)
        ctx.dma("sync", wbu[slot][:].rearrange("p (c o) -> p c o", o=128),
                uw_d[:, ot * 128:(ot + 1) * 128].rearrange("(c p) o -> p c o", p=128), sem="f1l%d" % slot)
        ctx.wait_now("vector", "f1l%d" % slot)
        ctx.wait("vector", "f1m", mm_done[slot])
        ctx.op("vector", lambda v, s=slot: v.tensor_copy(out=wsg[s][:], in_=wbg[s][:]))
        ctx.op("vector", lambda v, s=slot: v.tensor_copy(out=wsu[s][:], in_=wbu[s][:]),
               inc=("f1c", 1))
        ctx.wait_now("tensor", "f1c")
        ctx.wait_now("tensor", "f1e")
        for tg in range(2):
            for f in range(16):
                ctx.op("tensor", lambda te, s=slot, f=f, tg=tg: te.matmul(
                    pg[tg][:], wsg[s][:, f * 128:(f + 1) * 128],
                    x2T[:, f * 1024 + tg * 512: f * 1024 + (tg + 1) * 512],
                    start=(f == 0), stop=(f == 15)), inc=None)
            for f in range(16):
                ctx.op("tensor", lambda te, s=slot, f=f, tg=tg: te.matmul(
                    pu[tg][:], wsu[s][:, f * 128:(f + 1) * 128],
                    x2T[:, f * 1024 + tg * 512: f * 1024 + (tg + 1) * 512],
                    start=(f == 0), stop=(f == 15)),
                    inc=("f1m", 1) if (tg == 1 and f == 15) else None)
        mm_done[slot] = ctx.count("f1m")
        ctx.wait_now("scalar", "f1m")
        for tg in range(2):
            ctx.op("scalar", lambda s, tg=tg, ot=ot: s.activation(
                out=silug[:, tg * 512:(tg + 1) * 512], in_=pg[tg][:], func=AF.Silu,
                scale=gates[:, ot:ot + 1]), inc=("f1sg", 1))
        ctx.wait_now("vector", "f1sg")
        ctx.wait_now("vector", "f1sp")
        for tg in range(2):
            ctx.op("vector", lambda v, tg=tg, ot=ot: v.scalar_tensor_tensor(
                out=ptile[:, tg * 512:(tg + 1) * 512], in0=pu[tg][:], scalar=ups[:, ot:ot + 1],
                in1=silug[:, tg * 512:(tg + 1) * 512], op0=ALU.mult, op1=ALU.mult),
                inc=("f1e", 1))
        ctx.wait_now("gpsimd", "f1e")
        ctx.dma("gpsimd", p1_d[ot * 128:(ot + 1) * 128, :], ptile[:], sem="f1sp")
    for e in Ctx.ENGINES:
        ctx.wait_now(e, "f1sp")
    sA.close()

    with ctx.scope() as s2:
        sb2 = lambda n, s, d: s2.enter_context(nc.sbuf_tensor(n, s, d))
        dw = [sb2("f1dw%d" % i, [128, 512], F32) for i in range(2)]
        ds = [sb2("f1ds%d" % i, [128, 512], BF16) for i in range(2)]
        ptl = [sb2("f1ptl%d" % i, [128, 1024], BF16) for i in range(2)]
        dtile = sb2("f1_dtile", [128, 512], F32)
        g1t = sb2("f1_g1t", [128, 512], BF16)
        pd = [s2.enter_context(nc.psum_tensor("f1pd%d" % t, [128, 512], F32)) for t in range(8)]
        mmd = [0, 0]
        for og in range(4):
            for i in range(44):
                slot = i % 2
                ctx.wait("sync", "d1m", mmd[slot])
                ctx.dma("sync", dw[slot][:], dw_d[i * 128:(i + 1) * 128, og * 512:(og + 1) * 512],
                        sem="d1l%d" % slot)
                ctx.dma("sync", ptl[slot][:], p1_d[i * 128:(i + 1) * 128, :], sem="d1l%d" % slot)
                ctx.wait_now("vector", "d1l%d" % slot)
                ctx.wait("vector", "d1m", mmd[slot])
                ctx.op("vector", lambda v, s=slot: v.tensor_copy(out=ds[s][:], in_=dw[s][:]),
                       inc=("d1c", 1))
                ctx.wait_now("tensor", "d1c")
                if i == 0:
                    ctx.wait_now("tensor", "d1e")
                for t in range(8):
                    ctx.op("tensor", lambda te, s=slot, t=t, i=i: te.matmul(
                        pd[t][:], ptl[s][:, t * 128:(t + 1) * 128], ds[s][:],
                        start=(i == 0), stop=(i == 43)),
                        inc=("d1m", 1) if t == 7 else None)
                mmd[slot] = ctx.count("d1m")
            ctx.wait_now("vector", "d1m")
            for t in range(8):
                ctx.wait_now("vector", "d1sp")
                ctx.op("vector", lambda v, t=t, og=og: v.tensor_tensor(
                    out=dtile[:], in0=pd[t][:], in1=downs[:, og * 512:(og + 1) * 512],
                    op=ALU.mult), inc=("d1e", 1))
                ctx.wait_now("scalar", "d1e")
                ctx.op("scalar", lambda s: s.activation(out=g1t[:], in_=dtile[:], func=AF.Silu),
                       inc=("d1s", 1))
                ctx.wait_now("gpsimd", "d1s")
                ctx.dma("gpsimd", g["g1_d"][t * 128:(t + 1) * 128, og * 512:(og + 1) * 512],
                        g1t[:], sem="d1sp")
    for e in Ctx.ENGINES:
        ctx.wait_now(e, "d1sp")


def emit_P6_attn(ctx, g, scope):
    nc = ctx.nc
    xg = g["xn2grp_d"]
    wq_d, wk_d, wv_d, wgh_d = g["wq_d"], g["wk_d"], g["wv_d"], g["wgh_d"]
    onescol = g["onescol_sb"]
    idn = g["idn"]
    logg_d = g["logg_d"]
    y_out = g["y_out"]

    sb = lambda n, s, d: scope.enter_context(nc.sbuf_tensor(n, s, d))
    attnmask = sb("attnmask_sb", [128, 4 * 512], BF16)
    emit_load_consts(ctx, [(attnmask[:], g["am_d"])])
    q_sb = g["at_q"]
    k_sb = g["at_k"]
    vtm = g["at_vtm"]
    y_sb = g["at_y"]
    lgrow = sb("at_lgrow", [1, 4096], F32)
    lgT = sb("at_lgT", [128, 64], F32)
    sca = sb("at_sca", [128, 33], F32)
    f1r = sb("at_f1", [1, 512], F32)
    dsc = sb("at_dsc", [1, 3 * 512], F32)
    aT = [sb("at_aT%d" % i, [128, 512], BF16) for i in range(2)]

    with ctx.scope() as s1:
        sb1 = lambda n, s, d: s1.enter_context(nc.sbuf_tensor(n, s, d))
        wq16 = sb1("at_wq16", [128, 16 * 512], BF16)
        wk16 = sb1("at_wk16", [128, 16 * 256], BF16)
        wv16 = sb1("at_wv16", [128, 16 * 256], BF16)
        wg16 = sb1("at_wg16", [128, 16 * 2], BF16)
        lg = sb1("at_lg", [2, 4096], F32)
        xt = [sb1("at_xt%d" % i, [128, 512], BF16) for i in range(2)]
        s1v = ctx.scope()
        vfm = s1v.enter_context(nc.sbuf_tensor("at_vfm", [128, 2 * 4096], BF16))
        s1q = ctx.scope()
        pq = [s1q.enter_context(nc.psum_tensor("at_pq%d" % i, [128, 512], F32)) for i in range(4)]
        pk = [s1q.enter_context(nc.psum_tensor("at_pk%d" % i, [128, 512], F32)) for i in range(2)]
        pv = [s1q.enter_context(nc.psum_tensor("at_pv%d" % i, [128, 512], F32)) for i in range(2)]
        with ctx.scope() as s1a:
            wtmp = [s1a.enter_context(nc.sbuf_tensor("at_wt%d" % i, [128, 1026], F32))
                    for i in range(2)]
            for c in range(16):
                sl = c % 2
                if c >= 2:
                    ctx.wait("sync", "awc", (c - 1) * 4)
                ctx.dma("sync", wtmp[sl][:, 0:512], wq_d[c * 128:(c + 1) * 128, :], sem="awl%d" % sl)
                ctx.dma("sync", wtmp[sl][:, 512:768], wk_d[c * 128:(c + 1) * 128, :], sem="awl%d" % sl)
                ctx.dma("sync", wtmp[sl][:, 768:1024], wv_d[c * 128:(c + 1) * 128, :], sem="awl%d" % sl)
                ctx.dma("sync", wtmp[sl][:, 1024:1026], wgh_d[c * 128:(c + 1) * 128, :], sem="awl%d" % sl)
                ctx.wait_now("vector", "awl%d" % sl)
                ctx.op("vector", lambda v, c=c, sl=sl: v.tensor_copy(
                    out=wq16[:, c * 512:(c + 1) * 512], in_=wtmp[sl][:, 0:512]))
                ctx.op("vector", lambda v, c=c, sl=sl: v.tensor_copy(
                    out=wk16[:, c * 256:(c + 1) * 256], in_=wtmp[sl][:, 512:768]))
                ctx.op("vector", lambda v, c=c, sl=sl: v.tensor_copy(
                    out=wv16[:, c * 256:(c + 1) * 256], in_=wtmp[sl][:, 768:1024]))
                ctx.op("vector", lambda v, c=c, sl=sl: v.tensor_copy(
                    out=wg16[:, c * 2:(c + 1) * 2], in_=wtmp[sl][:, 1024:1026]),
                    inc=("awc", 4))
            for e in Ctx.ENGINES:
                ctx.wait_now(e, "awc")
        mm_at = 0
        for tg in (range(8) if "proj" in A_STAGES else []):
            r, toff = tg // 2, (tg % 2) * 512
            for f in range(16):
                slot = f % 2
                ctx.wait("sync", "apm", max(0, mm_at - 1))
                ctx.dma("sync", xt[slot][:], xg[r, f * 128:(f + 1) * 128, toff:toff + 512],
                        sem="axl%d" % slot)
                ctx.wait_now("tensor", "axl%d" % slot)
                if f < 2:
                    ctx.wait_now("tensor", "aev")
                for qh in range(4):
                    ctx.op("tensor", lambda te, s=slot, f=f, qh=qh: te.matmul(
                        pq[qh][:], wq16[:, f * 512 + qh * 128: f * 512 + (qh + 1) * 128],
                        xt[s][:], start=(f == 0), stop=(f == 15)), inc=None)
                for kv in range(2):
                    ctx.op("tensor", lambda te, s=slot, f=f, kv=kv: te.matmul(
                        pk[kv][:], wk16[:, f * 256 + kv * 128: f * 256 + (kv + 1) * 128],
                        xt[s][:], start=(f == 0), stop=(f == 15)), inc=None)
                    ctx.op("tensor", lambda te, s=slot, f=f, kv=kv: te.matmul(
                        pv[kv][:], wv16[:, f * 256 + kv * 128: f * 256 + (kv + 1) * 128],
                        xt[s][:], start=(f == 0), stop=(f == 15)),
                        inc=("apm", 1) if kv == 1 else None)
                mm_at = ctx.count("apm")
            ctx.wait_now("vector", "apm")
            for qh in range(4):
                ctx.op("vector", lambda v, qh=qh, tg=tg: v.tensor_copy(
                    out=q_sb[:, qh * 4096 + tg * 512: qh * 4096 + (tg + 1) * 512],
                    in_=pq[qh][:]), inc=None)
            for kv in range(2):
                ctx.op("vector", lambda v, kv=kv, tg=tg: v.tensor_copy(
                    out=k_sb[:, kv * 4096 + tg * 512: kv * 4096 + (tg + 1) * 512],
                    in_=pk[kv][:]), inc=None)
                ctx.op("vector", lambda v, kv=kv, tg=tg: v.tensor_copy(
                    out=vfm[:, kv * 4096 + tg * 512: kv * 4096 + (tg + 1) * 512],
                    in_=pv[kv][:]), inc=None)
            ctx.op("vector", lambda v: v.engine_nop(), inc=("aev", 1), compute=False)
        for e in Ctx.ENGINES:
            ctx.wait_now(e, "aev")
        s1q.close()
        with ctx.scope() as s1g:
            pg1 = s1g.enter_context(nc.psum_tensor("at_pg1", [2, 512], F32))
            gm = 0
            for tg in (range(8) if "gpass" in A_STAGES else []):
                r, toff = tg // 2, (tg % 2) * 512
                for f in range(16):
                    slot = f % 2
                    ctx.wait("sync", "agm", max(0, gm - 1))
                    ctx.dma("sync", xt[slot][:], xg[r, f * 128:(f + 1) * 128, toff:toff + 512],
                            sem="agl%d" % slot)
                    ctx.wait_now("tensor", "agl%d" % slot)
                    if f == 0:
                        ctx.wait_now("tensor", "agv")
                    ctx.op("tensor", lambda te, s=slot, f=f: te.matmul(
                        pg1[:], wg16[:, f * 2:(f + 1) * 2], xt[s][:],
                        start=(f == 0), stop=(f == 15)), inc=("agm", 1))
                    gm = ctx.count("agm")
                ctx.wait_now("vector", "agm")
                ctx.op("vector", lambda v, tg=tg: v.tensor_copy(
                    out=lg[0:2, tg * 512:(tg + 1) * 512], in_=pg1[0:2, :]), inc=("agv", 1))
            for e in Ctx.ENGINES:
                ctx.wait_now(e, "agv")
        ctx.wait_now("tensor", "aev")
        pvt = [s1.enter_context(nc.psum_tensor("at_pvt%d" % i, [128, 1024], BF16)) for i in range(2)]
        for kv in (range(2) if "vtr" in A_STAGES else []):
            for kt in range(32):
                pt = pvt[kt % 2]
                ctx.op("tensor", lambda te, kv=kv, kt=kt, pt=pt: te.transpose(
                    pt[:, 0:128], vfm[:, kv * 4096 + kt * 128: kv * 4096 + (kt + 1) * 128],
                    idn[:]), inc=("avt", 1))
                ctx.wait_now("vector", "avt")
                ctx.op("vector", lambda v, kv=kv, kt=kt, pt=pt: v.tensor_copy(
                    out=vtm[:, (kv * 32 + kt) * 128:(kv * 32 + kt + 1) * 128],
                    in_=pt[:, 0:128]), inc=("ave", 1))
                ctx.wait_now("tensor", "ave")
        for e in Ctx.ENGINES:
            ctx.wait_now(e, "ave")
        s1v.close()
        lgz = s1.enter_context(nc.sbuf_tensor("at_lgz", [32, 4096], F32))
        if "logg" not in A_STAGES:
            ctx.op("vector", lambda v: v.memset(lg[0:2, :], 0.0), inc=("alv", 1))
        ctx.op("vector", lambda v: v.tensor_scalar(out=lg[0:2, :], in0=lg[0:2, :],
                                                   scalar1=GATE_OFFSET, scalar2=None,
                                                   op0=ALU.add), inc=("alv", 1))
        ctx.wait_now("scalar", "alv")
        ctx.op("scalar", lambda s: s.activation(out=lgz[0:2, :], in_=lg[0:2, :], func=AF.Abs))
        ctx.op("scalar", lambda s: s.activation(out=lgz[0:2, :], in_=lgz[0:2, :], func=AF.Exp,
                                                scale=-1.0))
        ctx.op("scalar", lambda s: s.activation(out=lgz[0:2, :], in_=lgz[0:2, :], func=AF.Ln,
                                                bias=1.0), inc=("als", 1))
        ctx.wait_now("vector", "als")
        ctx.op("vector", lambda v: v.tensor_scalar(out=lg[0:2, :], in0=lg[0:2, :],
                                                   scalar1=0.0, scalar2=None, op0=ALU.min))
        ctx.op("vector", lambda v: v.tensor_sub(out=lg[0:2, :], in0=lg[0:2, :], in1=lgz[0:2, :]))
        ctx.op("vector", lambda v: v.memset(lgz[:], 0.0))
        ctx.op("vector", lambda v: v.tensor_tensor_scan(out=lgz[0:2, :], data0=lg[0:2, :],
                                                        data1=lgz[0:2, :], initial=0.0,
                                                        op0=ALU.add, op1=ALU.add), inc=("alv", 1))
        ctx.wait_now("gpsimd", "alv")
        ctx.dma("gpsimd", logg_d[:], lgz[0:2, :], sem="alsp")
        ctx.wait_now("sync", "alsp")
        ctx.wait_now("tensor", "alv")
        pvtf = s1.enter_context(nc.psum_tensor("at_pvtf", [128, 512], F32))
        for kt in range(32):
            ctx.op("tensor", lambda te, kt=kt: te.transpose(
                pvtf[:, 0:32], lgz[:, kt * 128:(kt + 1) * 128],
                g["idnf"][0:32, 0:32]), inc=("avt", 1))
            ctx.wait_now("vector", "avt")
            ctx.op("vector", lambda v, kt=kt: v.tensor_copy(
                out=lgT[:, kt * 2:(kt + 1) * 2], in_=pvtf[:, 0:2]),
                inc=("ave", 1))
            ctx.wait_now("tensor", "ave")
        for e in Ctx.ENGINES:
            ctx.wait_now(e, "ave")
            ctx.wait_now(e, "alr")

    if DBG_C:
        ctx.dma("gpsimd", g["dbg_q"], q_sb[:], sem="dbgc")
        ctx.dma("gpsimd", g["dbg_wq"], wq16[:], sem="dbgc")
        ctx.dma("gpsimd", g["dbg_xt"], xt[1][:], sem="dbgc")
        ctx.dma("gpsimd", g["dbg_k"], k_sb[:], sem="dbgc")
        ctx.dma("gpsimd", g["dbg_lg"], lg[0:2, :], sem="dbgc")
        ctx.dma("gpsimd", g["dbg_lgT"], lgT[:], sem="dbgc")
        for e in Ctx.ENGINES:
            ctx.wait_now(e, "dbgc")
    if "core" not in A_STAGES:
        return
    with ctx.scope() as s2:
        pst = [s2.enter_context(nc.psum_tensor("at_pst%d" % i, [128, 512], F32)) for i in range(2)]
        py = s2.enter_context(nc.psum_tensor("at_py", [128, 512], F32))
        pden = s2.enter_context(nc.psum_tensor("at_pden", [1, 512], F32))
        prb = s2.enter_context(nc.psum_tensor("at_prb", [128, 512], F32))
        rb = scope.enter_context(nc.sbuf_tensor("at_rb", [128, 512], F32))
        LN128 = float(np.log(128.0))
        for qh in range(4):
            kv = qh // 2
            if qh % 2 == 0:
                ctx.wait_now("sync", "asc")
                ctx.wait_now("sync", "asv")
                ctx.dma("sync", lgrow[:], logg_d[kv:kv + 1, :], sem="alr")
                ctx.wait_now("vector", "alr")
                ctx.wait_now("scalar", "alr")
                ctx.wait_now("tensor", "alr")
            for qg in range(8):
                nkt = qg * 4 + 4
                mq = lgrow[0:1, qg * 512: qg * 512 + 1]
                ctx.wait_now("tensor", "asc")
                ctx.op("tensor", lambda te, mq=mq: te.matmul(
                    prb[:, 0:1], g["onesf_sb"][0:1, :], mq, start=True, stop=True),
                    inc=("amq", 1))
                ctx.wait_now("vector", "amq")
                ctx.wait_now("vector", "asc")
                ctx.op("vector", lambda v: v.tensor_copy(out=sca[:, 32:33], in_=prb[:, 0:1]))
                ctx.op("vector", lambda v, kv=kv, nkt=nkt: v.tensor_scalar(
                    out=sca[:, 0:nkt], in0=lgT[:].rearrange("p (kt h) -> p h kt", h=2)[:, kv, 0:nkt],
                    scalar1=sca[:, 32:33], scalar2=LN128, op0=ALU.subtract, op1=ALU.add),
                    inc=("asv", 1))
                ctx.wait_now("scalar", "asv")
                ctx.op("scalar", lambda s, nkt=nkt: s.activation(
                    out=sca[:, 0:nkt], in_=sca[:, 0:nkt], func=AF.Exp, scale=-0.5),
                    inc=("asc", 1))
                ctx.op("vector", lambda v, kv=kv, qg=qg, mq=mq: v.tensor_single_scalar(
                    out=f1r[:], in_=lgrow[0:1, qg * 512:(qg + 1) * 512],
                    scalar=mq, op=ALU.subtract), inc=("asv", 1))
                ctx.wait_now("scalar", "asv")
                ctx.op("scalar", lambda s: s.activation(out=f1r[:], in_=f1r[:], func=AF.Exp),
                       inc=("asc", 1))
                ctx.wait_now("tensor", "asc")
                for kt in range(nkt):
                    sl = kt % 2
                    ctx.op("tensor", lambda te, kv=kv, kt=kt, qh=qh, qg=qg, sl=sl: te.matmul(
                        pst[sl][:], k_sb[:, kv * 4096 + kt * 128: kv * 4096 + (kt + 1) * 128],
                        q_sb[:, qh * 4096 + qg * 512: qh * 4096 + (qg + 1) * 512],
                        start=True, stop=True), inc=("amm", 1))
                    ctx.wait_now("scalar", "amm")
                    ctx.wait("scalar", "am2", max(0, ctx.count("am2")))
                    ctx.op("scalar", lambda s, kt=kt, sl=sl: s.activation(
                        out=aT[sl][:], in_=pst[sl][:], func=AF.Square,
                        scale=sca[:, kt:kt + 1]), inc=("asq", 1))
                    if kt >= qg * 4:
                        ctx.wait_now("vector", "asq")
                        ctx.op("vector", lambda v, kt=kt, qg=qg, sl=sl: v.tensor_tensor(
                            out=aT[sl][:], in0=aT[sl][:],
                            in1=attnmask[:, (kt - qg * 4) * 512:(kt - qg * 4 + 1) * 512],
                            op=ALU.mult), inc=("amk", 1))
                        ctx.wait_now("tensor", "amk")
                    else:
                        ctx.wait_now("tensor", "asq")
                    ctx.op("tensor", lambda te, kv=kv, kt=kt, sl=sl, first=(kt == 0), last=(kt == nkt - 1): te.matmul(
                        py[:], vtm[:, (kv * 32 + kt) * 128:(kv * 32 + kt + 1) * 128],
                        aT[sl][:], start=first, stop=last), inc=None)
                    ctx.op("tensor", lambda te, kt=kt, sl=sl, first=(kt == 0), last=(kt == nkt - 1): te.matmul(
                        pden[:], onescol[:], aT[sl][:], start=first, stop=last),
                        inc=("am2", 1))
                ctx.wait_now("vector", "am2")
                ctx.op("vector", lambda v: v.tensor_copy(out=dsc[:, 0:512], in_=pden[:]))
                ctx.op("vector", lambda v: v.reciprocal(out=dsc[:, 512:1024], in_=f1r[:]))
                ctx.op("vector", lambda v: v.scalar_tensor_tensor(
                    out=dsc[:, 1024:1536], in0=dsc[:, 512:1024], scalar=1e-6,
                    in1=dsc[:, 0:512], op0=ALU.mult, op1=ALU.add))
                ctx.op("vector", lambda v: v.reciprocal(out=dsc[:, 1024:1536], in_=dsc[:, 1024:1536]),
                       inc=("arc", 1))
                ctx.wait_now("tensor", "arc")
                ctx.op("tensor", lambda te: te.matmul(
                    prb[:], g["onesf_sb"][0:1, :], dsc[0:1, 1024:1536],
                    start=True, stop=True), inc=("arb", 1))
                ctx.wait_now("vector", "arb")
                ctx.op("vector", lambda v: v.tensor_copy(out=rb[:], in_=prb[:]))
                ctx.op("vector", lambda v, qh=qh, qg=qg: v.tensor_tensor(
                    out=y_sb[:, qh * 4096 + qg * 512: qh * 4096 + (qg + 1) * 512],
                    in0=py[:], in1=rb[:], op=ALU.mult), inc=("asc", 1))
        for e in Ctx.ENGINES:
            ctx.wait_now(e, "asc")
    for qh in range(4):
        ctx.dma("gpsimd", y_out[qh * 128:(qh + 1) * 128, :],
                y_sb[:, qh * 4096:(qh + 1) * 4096], sem="aysp")
    for e in Ctx.ENGINES:
        ctx.wait_now(e, "aysp")


C_INPUTS = ["xn2grp", "xn2T_own", "wq_loc", "wk_loc", "wv_loc", "wg_loc",
            "gate_w", "up_w", "down_w", "gates_t", "ups_t", "downs_bc",
            "attnmask", "idn128", "idn128f", "ones1x128f", "onescol_bf"]


def build_launchC():
    nc = bass.Bass("TRN2", target_bir_lowering=False)
    with ExitStack() as es:
        ctx = Ctx(nc, es)
        g = {}
        g["xn2grp_d"] = decl_in(nc, "xn2grp", [4, D, TOK], BF16)
        xn2own = decl_in(nc, "xn2T_own", [D, TOK], BF16)
        g["wq_d"] = decl_in(nc, "wq_loc", [D, 512], F32)
        g["wk_d"] = decl_in(nc, "wk_loc", [D, 256], F32)
        g["wv_d"] = decl_in(nc, "wv_loc", [D, 256], F32)
        g["wgh_d"] = decl_in(nc, "wg_loc", [D, 2], F32)
        g["gate_w_d"] = decl_in(nc, "gate_w", [D, FF1], F32)
        g["up_w_d"] = decl_in(nc, "up_w", [D, FF1], F32)
        g["down_w_d"] = decl_in(nc, "down_w", [FF1, D], F32)
        gates_d = decl_in(nc, "gates_t", [128, 44], F32)
        ups_d = decl_in(nc, "ups_t", [128, 44], F32)
        downs_d = decl_in(nc, "downs_bc", [128, D], F32)
        am_d = decl_in(nc, "attnmask", [128, 4 * 512], BF16)
        idn_d = decl_in(nc, "idn128", [128, 128], BF16)
        idnf_d = decl_in(nc, "idn128f", [128, 128], F32)
        onesf_d = decl_in(nc, "ones1x128f", [1, 128], F32)
        onesc_d = decl_in(nc, "onescol_bf", [128, 1], BF16)
        g["y_out"] = decl_out(nc, "y_out", [512, L], BF16)
        g["g1_d"] = decl_out(nc, "g1_d", [TOK, D], BF16)
        g["p1_d"] = decl_scratch(nc, "p1_d", [FF1, TOK], BF16, False)
        if DBG_C:
            g["dbg_wq"] = decl_out(nc, "dbg_wq", [128, 16 * 512], BF16)
            g["dbg_xt"] = decl_out(nc, "dbg_xt", [128, 512], BF16)
            g["dbg_q"] = decl_out(nc, "dbg_q", [128, 4 * 4096], BF16)
            g["dbg_k"] = decl_out(nc, "dbg_k", [128, 2 * 4096], BF16)
            g["dbg_lg"] = decl_out(nc, "dbg_lg", [2, 4096], F32)
            g["dbg_lgT"] = decl_out(nc, "dbg_lgT", [128, 64], F32)
        g["logg_d"] = decl_scratch(nc, "logg_d", [2, L], F32, True)

        g["idn"] = ctx.sb("idn", [128, 128], BF16)
        g["idnf"] = ctx.sb("idnf", [128, 128], F32)
        g["onesf_sb"] = ctx.sb("onesf_sb", [1, 128], F32)
        g["onescol_sb"] = ctx.sb("onescol_sb", [128, 1], BF16)
        g["gates_d"], g["ups_d"], g["downs_d"], g["am_d"] = gates_d, ups_d, downs_d, am_d
        emit_load_consts(ctx, [
            (g["idn"][:], idn_d), (g["idnf"][:], idnf_d), (g["onesf_sb"][:], onesf_d),
            (g["onescol_sb"][:], onesc_d),
        ])
        g["at_q"] = ctx.sb("at_q", [128, 4 * 4096], BF16)
        g["at_k"] = ctx.sb("at_k", [128, 2 * 4096], BF16)
        g["at_vtm"] = ctx.sb("at_vtm", [128, 2 * 32 * 128], BF16)
        g["at_y"] = ctx.sb("at_y", [128, 4 * 4096], BF16)
        with ctx.scope() as s0:
            g["x2T_sb"] = s0.enter_context(nc.sbuf_tensor("x2T", [128, 16 * 1024], BF16))
            for f in range(16):
                ctx.dma("sync", g["x2T_sb"][:, f * 1024:(f + 1) * 1024],
                        xn2own[f * 128:(f + 1) * 128, :], sem="x2l")
            for e in Ctx.ENGINES:
                ctx.wait_now(e, "x2l")
            if "ffn1" in C_STAGES:
                emit_P5_ffn1(ctx, g, s0)
        if "attn" in C_STAGES:
            with ctx.scope() as s1:
                emit_P6_attn(ctx, g, s1)
        ctx.emit()
    return nc


D_INPUTS = ["yall_loc", "x2_loc", "g1_loc", "wc", "bc_bc", "idn128"]


def build_launchD():
    nc = bass.Bass("TRN2", target_bir_lowering=False)
    with ExitStack() as es:
        ctx = Ctx(nc, es)
        g = {}
        yall = decl_in(nc, "yall_loc", [D, TOK], BF16)
        x2_d = decl_in(nc, "x2_loc", [TOK, D], F32)
        g1_d = decl_in(nc, "g1_loc", [TOK, D], BF16)
        wc_d = decl_in(nc, "wc", [D, D], F32)
        bcb_d = decl_in(nc, "bc_bc", [128, D], F32)
        out_d = decl_out(nc, "out_loc", [TOK, D], F32)

        bcb = ctx.sb("bcb", [128, D], F32)
        emit_load_consts(ctx, [(bcb[:], bcb_d)])
        with ctx.scope() as s0:
            sb = lambda n, s, d: s0.enter_context(nc.sbuf_tensor(n, s, d))
            yT = sb("d_yT", [128, 16 * 1024], BF16)
            g1 = sb("d_g1", [128, 8 * 2048], BF16)
            wcb = [sb("d_wc%d" % i, [128, 512], F32) for i in range(2)]
            wcb16 = [sb("d_wc16%d" % i, [128, 512], BF16) for i in range(2)]
            xtl = sb("d_xtl", [128, 512], F32)
            ot = sb("d_ot", [128, 512], F32)
            po = [s0.enter_context(nc.psum_tensor("d_po%d" % t, [128, 512], F32)) for t in range(8)]
            for f in range(16):
                ctx.dma("sync", yT[:, f * 1024:(f + 1) * 1024],
                        yall[f * 128:(f + 1) * 128, :], sem="dl")
            for t in range(8):
                ctx.dma("sync", g1[:, t * 2048:(t + 1) * 2048],
                        g1_d[t * 128:(t + 1) * 128, :], sem="dl")
            for e in Ctx.ENGINES:
                ctx.wait_now(e, "dl")
            mmd = [0, 0]
            for og in range(4):
                for f in range(16):
                    slot = f % 2
                    ctx.wait("sync", "dm", mmd[slot])
                    ctx.dma("sync", wcb[slot][:], wc_d[f * 128:(f + 1) * 128, og * 512:(og + 1) * 512],
                            sem="dwl%d" % slot)
                    ctx.wait_now("vector", "dwl%d" % slot)
                    ctx.wait("vector", "dm", mmd[slot])
                    ctx.op("vector", lambda v, s=slot: v.tensor_copy(out=wcb16[s][:], in_=wcb[s][:]),
                           inc=("dc", 1))
                    ctx.wait_now("tensor", "dc")
                    if f == 0:
                        ctx.wait_now("tensor", "dev")
                    for t in range(8):
                        ctx.op("tensor", lambda te, s=slot, f=f, t=t: te.matmul(
                            po[t][:], yT[:, f * 1024 + t * 128: f * 1024 + (t + 1) * 128],
                            wcb16[s][:], start=(f == 0), stop=(f == 15)),
                            inc=("dm", 1) if t == 7 else None)
                    mmd[slot] = ctx.count("dm")
                for t in range(8):
                    ctx.wait_now("sync", "dev")
                    ctx.dma("sync", xtl[:], x2_d[t * 128:(t + 1) * 128, og * 512:(og + 1) * 512],
                            sem="dxl")
                    ctx.wait_now("vector", "dxl")
                    ctx.wait_now("vector", "dm")
                    ctx.wait_now("vector", "dsp")
                    ctx.op("vector", lambda v, t=t, og=og: v.tensor_tensor(
                        out=ot[:], in0=po[t][:], in1=bcb[:, og * 512:(og + 1) * 512], op=ALU.add))
                    ctx.op("vector", lambda v, t=t, og=og: v.tensor_tensor(
                        out=ot[:], in0=ot[:], in1=g1[:, t * 2048 + og * 512: t * 2048 + (og + 1) * 512],
                        op=ALU.mult))
                    ctx.op("vector", lambda v: v.tensor_add(out=ot[:], in0=ot[:], in1=xtl[:]),
                           inc=("dev", 1))
                    ctx.wait_now("gpsimd", "dev")
                    ctx.dma("gpsimd", out_d[t * 128:(t + 1) * 128, og * 512:(og + 1) * 512],
                            ot[:], sem="dsp")
            for e in Ctx.ENGINES:
                ctx.wait_now(e, "dsp")
        ctx.emit()
    return nc


def run_launchC(in_maps, resB):
    nc = build_launchC()
    xn2T_all = [np.asarray(r["xn2T_d"]) for r in resB]
    ims = []
    for k in range(8):
        b, j = k // 4, k % 4
        im = {n: in_maps[k][n] for n in ("wq_loc", "wk_loc", "wv_loc", "wg_loc",
                                         "gate_w", "up_w", "down_w", "gates_t", "ups_t",
                                         "downs_bc", "attnmask", "idn128", "idn128f",
                                         "ones1x128f", "onescol_bf")}
        im["xn2grp"] = np.stack([xn2T_all[b * 4 + m] for m in range(4)])
        im["xn2T_own"] = xn2T_all[k]
        ims.append(im)
    res = run_bass_kernel_spmd(nc, ims, core_ids=list(range(8)))
    return res.results


def run_launchD(in_maps, resB, resC):
    nc = build_launchD()
    ims = []
    for k in range(8):
        b, j = k // 4, k % 4
        yall = np.concatenate([np.asarray(resC[b * 4 + m]["y_out"]) for m in range(4)], axis=0)
        im = dict(
            yall_loc=np.ascontiguousarray(yall[:, j * 1024:(j + 1) * 1024]),
            x2_loc=np.asarray(resB[k]["x2_d"]),
            g1_loc=np.asarray(resC[k]["g1_d"]),
            wc=in_maps[k]["wc"], bc_bc=in_maps[k]["bc_bc"], idn128=in_maps[k]["idn128"],
        )
        ims.append(im)
    res = run_bass_kernel_spmd(nc, ims, core_ids=list(range(8)))
    return res.results
